# version 07
# baseline (speedup 1.0000x reference)
import contextlib
import numpy as np
import ml_dtypes
import concourse.bass as bass
import concourse.mybir as mybir
from concourse.bass_utils import run_bass_kernel_spmd

F32 = mybir.dt.float32
BF16 = mybir.dt.bfloat16
AF = mybir.ActivationFunctionType
ALU = mybir.AluOpType
AX = mybir.AxisListType

B, S, D = 4, 2048, 2048
TOK = 1024
TEXT = 2048
DC = D // 128
DFF = 8192
NE, FMOE, KTOP = 16, 768, 2
EPS_RMS = 1.1920929e-07
EPS_LN = 1e-5
EPS_WC = 1e-6

ENGINES = ("sync", "scalar", "vector", "tensor", "gpsimd")
DEV_OUT = ()


class Sem:
    def __init__(self, name, dma=False):
        self.name = name
        self.dma = dma
        self.val = 0
        self.h = None

    def tok(self):
        return (self, self.val)


class Prog:

    def __init__(self):
        self.q = {e: [] for e in ENGINES}
        self.sems = []
        self.pending = {e: set() for e in ENGINES}
        self.last_w = {e: {} for e in ENGINES}
        self.hz = None

    def sem(self, name, dma=False):
        s = Sem(name, dma)
        self.sems.append(s)
        return s

    @staticmethod
    def _name(x):
        t = getattr(x, "tensor", None)
        if t is not None:
            return getattr(t, "name", None)
        if hasattr(x, "shape") and hasattr(x, "name"):
            return x.name
        return None

    def _names(self, xs):
        out = set()
        for x in xs:
            n = self._name(x)
            if n is not None:
                out.add(n)
        return out

    def op(self, eng, fn, waits=(), inc=None, inc_n=1, rd=(), wr=()):
        waits = list(w for w in waits if w is not None)
        rdn = self._names(rd)
        wrn = self._names(wr)
        if eng == "vector" and rdn & self.pending[eng]:
            self.q[eng].append(lambda eh: eh.drain())
            self.pending[eng] = set()
        elif eng == "scalar":
            hazard = rdn & self.pending[eng]
            for n in hazard:
                t = self.last_w[eng].get(n)
                if t is not None:
                    waits.append(t)
                self.pending[eng].discard(n)
            if wrn and inc is None:
                if self.hz is None:
                    self.hz = self.sem("hzs")
                inc = self.hz
                inc_n = 1
        for s, v in waits:
            assert not s.dma or v == s.val, (
                f"partial wait on dma sem {s.name}: {v} < {s.val}")
        if inc is not None:
            inc.val += inc_n

        def run(eh, fn=fn, waits=waits, inc=inc, inc_n=inc_n):
            for s, v in waits:
                if v > 0:
                    eh.wait_ge(s.h, v)
            instr = fn(eh) if fn is not None else None
            if inc is not None:
                assert instr is not None, "inc requires an instruction"
                instr.then_inc(inc.h, inc_n)
            return instr

        self.q[eng].append(run)
        self.pending[eng] |= wrn
        if eng == "scalar" and inc is not None:
            for n in wrn:
                self.last_w[eng][n] = inc.tok()
        return inc.tok() if inc is not None else None

    def dma(self, eng, out, in_, waits=(), inc=None):
        return self.op(eng, lambda eh, o=out, i=in_: eh.dma_start(out=o, in_=i),
                       waits=waits, inc=inc, inc_n=16)

    def dma_t(self, eng, out, in_, waits=(), inc=None):
        return self.op(eng, lambda eh, o=out, i=in_: eh.dma_start_transpose(out=o, in_=i),
                       waits=waits, inc=inc, inc_n=16)

    def tt(self, out, a, b, alu, waits=(), inc=None, eng="vector"):
        return self.op(eng, lambda eh, o=out, x=a, y=b, u=alu:
                       eh.tensor_tensor(o, x, y, op=u), waits=waits, inc=inc,
                       rd=[a, b], wr=[out])

    def ts(self, out, in0, s1, s2=None, op0=ALU.mult, op1=None, waits=(), inc=None,
           eng="vector", accum=None):
        kw = {}
        if op1 is not None:
            kw["op1"] = op1
        if accum is not None:
            kw["accum_out"] = accum
        return self.op(eng, lambda eh, o=out, i=in0, a=s1, b=s2, u=op0, k=kw:
                       eh.tensor_scalar(out=o, in0=i, scalar1=a, scalar2=b, op0=u, **k),
                       waits=waits, inc=inc, rd=[in0, s1, s2], wr=[out, accum])

    def copy(self, out, in_, waits=(), inc=None, eng="vector"):
        return self.op(eng, lambda eh, o=out, i=in_: eh.tensor_copy(o, i),
                       waits=waits, inc=inc, rd=[in_], wr=[out])

    def recip(self, out, in_, waits=(), inc=None):
        return self.op("vector", lambda eh, o=out, i=in_: eh.reciprocal(o, i),
                       waits=waits, inc=inc, rd=[in_], wr=[out])

    def red(self, out, in_, alu=ALU.add, waits=(), inc=None, negate=None):
        if alu == ALU.add:
            return self.op("vector", lambda eh, o=out, i=in_:
                           eh.reduce_sum(o, i, axis=AX.X), waits=waits, inc=inc,
                           rd=[in_], wr=[out])
        return self.op("vector", lambda eh, o=out, i=in_, u=alu, n=negate:
                       eh.tensor_reduce(o, i, axis=AX.X, op=u, negate=n),
                       waits=waits, inc=inc, rd=[in_], wr=[out])

    def scan(self, out, data0, data1, initial, op0, op1, waits=(), inc=None):
        return self.op("vector", lambda eh, o=out, d0=data0, d1=data1, i=initial,
                       u0=op0, u1=op1:
                       eh.tensor_tensor_scan(o, d0, d1, i, op0=u0, op1=u1),
                       waits=waits, inc=inc, rd=[data0, data1, initial], wr=[out])

    def memset(self, ap, val, waits=(), inc=None, eng="vector"):
        return self.op(eng, lambda eh, a=ap, v=val: eh.memset(a, v),
                       waits=waits, inc=inc, wr=[ap])

    def act(self, out, in_, func, bias=None, scale=None, accum=None, waits=(), inc=None):
        kw = {}
        if bias is not None:
            kw["bias"] = bias
        if scale is not None:
            kw["scale"] = scale
        if accum is not None:
            kw["accum_out"] = accum
        return self.op("scalar", lambda eh, o=out, i=in_, f=func, k=kw:
                       eh.activation(o, i, f, **k), waits=waits, inc=inc,
                       rd=[in_, bias, scale], wr=[out, accum])

    def mm(self, out, lhsT, rhs, start, stop, waits=(), inc=None):
        return self.op("tensor", lambda eh, o=out, l=lhsT, r=rhs, a=start, z=stop:
                       eh.matmul(o, l, r, start=a, stop=z), waits=waits, inc=inc)

    def wait(self, eng, *toks):
        self.op(eng, None, waits=toks)



def emit_p0(p, sb, dr, out):
    sd = [p.sem("p0d0", dma=True), p.sem("p0d1", dma=True)]
    sg = [p.sem("p0g0", dma=True), p.sem("p0g1", dma=True)]
    sv = p.sem("p0v")
    ss = p.sem("p0s")
    NT = TEXT // 128
    slot = [None, None]
    last_tmpa_reader = None
    last_stat_v = None
    st_all = sb["stats"]
    for tt in range(NT):
        bi = tt % 2
        own = tt >= NT // 2
        xt = sb["xt"][:, bi]
        st = st_all[:, bi * 24:(bi + 1) * 24]
        pv = slot[bi]
        tload = p.dma("sync", xt, dr["x_ext"][tt * 128:(tt + 1) * 128, :],
                      waits=([pv["t_xn"], pv["t_sum"]] if pv else []), inc=sd[bi])
        t_sq = p.act(sb["tmpa"][:], xt, AF.Square, accum=st[:, 0:1],
                     waits=[tload, last_tmpa_reader,
                            pv["t_stat"] if pv else None], inc=ss)
        t_sum = p.red(st[:, 1:2], xt,
                      waits=[tload] + ([pv["stores"], pv["t_xn"]] if pv else []),
                      inc=sv)
        p.ts(st[:, 2:3], st[:, 1:2], 1.0 / D)
        p.ts(st[:, 7:8], st[:, 0:1], 1.0 / D, waits=[t_sq])
        p.tt(st[:, 8:9], st[:, 2:3], st[:, 2:3], ALU.mult)
        p.tt(st[:, 9:10], st[:, 7:8], st[:, 8:9], ALU.subtract)
        t_u = p.ts(st[:, 10:11], st[:, 9:10], EPS_LN, op0=ALU.add, inc=sv)
        t_s0 = p.act(st[:, 12:13], st[:, 10:11], AF.Sqrt, waits=[t_u], inc=ss)
        p.recip(st[:, 3:4], st[:, 12:13], waits=[t_s0])
        for _ in range(2):
            p.tt(st[:, 13:14], st[:, 3:4], st[:, 3:4], ALU.mult)
            p.tt(st[:, 13:14], st[:, 13:14], st[:, 10:11], ALU.mult)
            p.ts(st[:, 13:14], st[:, 13:14], -0.5, 1.5, op0=ALU.mult, op1=ALU.add)
            p.tt(st[:, 3:4], st[:, 3:4], st[:, 13:14], ALU.mult)
        p.recip(st[:, 14:15], st[:, 10:11])
        p.tt(st[:, 15:16], st[:, 9:10], st[:, 14:15], ALU.mult)
        t_q = p.ts(st[:, 16:17], st[:, 15:16], EPS_RMS, op0=ALU.add, inc=sv)
        t_s1 = p.act(st[:, 17:18], st[:, 16:17], AF.Sqrt, waits=[t_q], inc=ss)
        p.recip(st[:, 4:5], st[:, 17:18], waits=[t_s1])
        for _ in range(2):
            p.tt(st[:, 13:14], st[:, 4:5], st[:, 4:5], ALU.mult)
            p.tt(st[:, 13:14], st[:, 13:14], st[:, 16:17], ALU.mult)
            p.ts(st[:, 13:14], st[:, 13:14], -0.5, 1.5, op0=ALU.mult, op1=ALU.add)
            p.tt(st[:, 4:5], st[:, 4:5], st[:, 13:14], ALU.mult)
        p.tt(st[:, 5:6], st[:, 10:11], st[:, 3:4], ALU.mult)
        p.tt(st[:, 6:7], st[:, 2:3], st[:, 3:4], ALU.mult)
        t_stat = p.ts(st[:, 6:7], st[:, 6:7], -1.0, inc=sv)
        t_xn = p.act(sb["tmpa"][:], xt, AF.Identity, bias=st[:, 6:7], scale=st[:, 3:4],
                     waits=[t_stat], inc=ss)
        t_xr = p.ts(sb["tmpb"][:], sb["tmpa"][:], st[:, 4:5], waits=[t_xn], inc=sv)
        t_xrh = p.copy(sb["hsta"][:, bi, 0], sb["tmpb"][:],
                       waits=[pv["stores"]] if pv else [], inc=sv)
        t_xrl = p.tt(sb["lsta"][:, bi, 0], sb["tmpb"][:], sb["hsta"][:, bi, 0],
                     ALU.subtract, inc=sv)
        trow = tt * 128
        p.dma("gpsimd", dr["xr_hi"][trow:trow + 128, :], sb["hsta"][:, bi, 0],
              waits=[t_xrh], inc=sg[bi])
        tok_st = p.dma("gpsimd", dr["xr_lo"][trow:trow + 128, :], sb["lsta"][:, bi, 0],
                       waits=[t_xrl], inc=sg[bi])
        last_tmpa = t_xr
        if own:
            orow = trow - TOK
            t_xnh = p.copy(sb["hsta"][:, bi, 1], sb["tmpa"][:], inc=sv)
            t_xnl = p.tt(sb["lsta"][:, bi, 1], sb["tmpa"][:], sb["hsta"][:, bi, 1],
                         ALU.subtract, inc=sv)
            p.dma("gpsimd", dr["xn_hi"][orow:orow + 128, :], sb["hsta"][:, bi, 1],
                  waits=[t_xnh], inc=sg[bi])
            p.dma("gpsimd", dr["xn_lo"][orow:orow + 128, :], sb["lsta"][:, bi, 1],
                  waits=[t_xnl], inc=sg[bi])
            p.dma("gpsimd", dr["mrow"][orow:orow + 128, :], st[:, 2:3],
                  waits=[t_stat], inc=sg[bi])
            tok_st = p.dma("gpsimd", dr["stdrow"][orow:orow + 128, :], st[:, 5:6],
                           waits=[t_stat], inc=sg[bi])
            last_tmpa = t_xnl
        slot[bi] = {"t_xn": t_xn, "t_sum": t_sum, "t_stat": t_stat, "stores": tok_st}
        last_tmpa_reader = last_tmpa
        last_stat_v = t_stat
        yield
    out["p0"] = [sg[0].tok(), sg[1].tok(), sv.tok(), ss.tok()]


def emit_prep(p, sb, dr, marks):
    sld = p.sem("prd", dma=True)
    sxb = p.sem("prx", dma=True)
    sth = [p.sem("prh0", dma=True), p.sem("prh1", dma=True)]
    stl = [p.sem("prl0", dma=True), p.sem("prl1", dma=True)]
    ssc = p.sem("prsc")
    svc = p.sem("prvc")

    def allstores():
        return (sth[0].tok(), sth[1].tok(), stl[0].tok(), stl[1].tok())

    marks["__sems__"] = (sth[0], sth[1], stl[0], stl[1])
    state = {"n": 0, "split": [None] * 4, "deferred": [], "stlast": [None, None]}


    def flush_deferred():
        for hb, fn in state["deferred"]:
            fn()
            state["stlast"][hb] = (sth[hb].tok(), stl[hb].tok())
        state["deferred"] = []

    def run_chunk(src_ap, ncols, dst_hi, dst_lo, w, xbar_dst=None):
        flush_deferred()
        cb = state["n"] % 4
        hb = state["n"] % 2
        state["n"] += 1
        ws = sb["wslab"][:, cb, :ncols]
        hi = sb["wshi"][:, hb, :ncols]
        lo = sb["wslo"][:, hb, :ncols]
        tl = p.dma("scalar", ws, src_ap, waits=[state["split"][cb]], inc=sld)
        th = p.act(hi, ws, AF.Identity,
                   waits=[tl] + list(state["stlast"][hb] or ()), inc=ssc)
        tlo = p.tt(lo, ws, hi, ALU.subtract, waits=[tl, th], inc=svc)
        state["split"][cb] = tlo
        if xbar_dst is None:
            def emit_stores(hi=hi, lo=lo, th=th, tlo=tlo, w=w, hb=hb,
                            dst_hi=dst_hi, dst_lo=dst_lo):
                p.dma("scalar", dst_hi, hi, waits=[th], inc=sth[hb])
                p.dma("scalar", dst_lo, lo, waits=[tlo], inc=stl[hb])
                marks[w] = allstores()
        else:
            def emit_stores(hi=hi, lo=lo, th=th, tlo=tlo, w=w, hb=hb, xd=xbar_dst):
                dh, dl = xd
                p.dma_t("sync", sb["xbout"][:, 0, :8], hi,
                        waits=[th] + list(allstores()), inc=sxb)
                p.dma_t("sync", sb["xbout"][:, 1, :8], lo,
                        waits=[tlo], inc=sxb)
                p.dma("scalar", dh, sb["xbout"][:, 0, :8],
                      waits=[(sxb, sxb.val)], inc=sth[hb])
                p.dma("scalar", dl, sb["xbout"][:, 1, :8],
                      waits=[(sxb, sxb.val)], inc=stl[hb])
                marks[w] = allstores()
        state["deferred"].append((hb, emit_stores))

    for fc in range(48):
        for j in range(2):
            cols = slice(j * 1024, (j + 1) * 1024)
            run_chunk(dr["w_qkv"][fc * 128:(fc + 1) * 128, cols], 1024, None, None,
                      "qkv",
                      xbar_dst=(dr["qkvT_hi"][fc][:, 8 * j:8 * j + 8, :],
                                dr["qkvT_lo"][fc][:, 8 * j:8 * j + 8, :]))
        flush_deferred()
        yield
    for e in range(NE):
        for rc in range(DC):
            rows = slice(rc * 128, (rc + 1) * 128)
            run_chunk(dr["w_gate"][e][rows, :], FMOE, dr["gateN_hi"][e][rows, :],
                      dr["gateN_lo"][e][rows, :], f"moe{e}")
            run_chunk(dr["w_up"][e][rows, :], FMOE, dr["upN_hi"][e][rows, :],
                      dr["upN_lo"][e][rows, :], f"moe{e}")
            yield
        for rc in range(6):
            rows = slice(rc * 128, (rc + 1) * 128)
            for j in range(2):
                cols = slice(j * 1024, (j + 1) * 1024)
                run_chunk(dr["w_down"][e][rows, cols], 1024,
                          dr["downN_hi"][e][rows, cols],
                          dr["downN_lo"][e][rows, cols], f"moe{e}")
            yield
        flush_deferred()
        marks[f"moe{e}done"] = True
    for fp in range(64):
        for rc in (fp, 64 + fp):
            rows = slice(rc * 128, (rc + 1) * 128)
            for j in range(2):
                cols = slice(j * 1024, (j + 1) * 1024)
                run_chunk(dr["w_sw"][rows, cols], 1024, dr["swN_hi"][rows, cols],
                          dr["swN_lo"][rows, cols], f"swg{fp // 8}")
        if fp % 8 == 7:
            flush_deferred()
            marks[f"swgdone{fp // 8}"] = marks[f"swg{fp // 8}"]
        yield
    for j in range(4):
        cols2 = slice(j * 2048, (j + 1) * 2048)
        for rc in range(DC):
            rows = slice(rc * 128, (rc + 1) * 128)
            for jj in range(2):
                cols = slice(j * 2048 + jj * 1024, j * 2048 + (jj + 1) * 1024)
                run_chunk(dr["w_out"][rows, cols], 1024, dr["woN_hi"][rows, cols],
                          dr["woN_lo"][rows, cols], f"woj{j}")
            yield
        flush_deferred()
        marks[f"wojdone{j}"] = marks[f"woj{j}"]
    flush_deferred()




def prep_toks(marks):
    return [(s, s.val) for s in marks["__sems__"]]

def emit_p1(p, sb, dr, p0_toks, marks, out):
    sxr = p.sem("p1xr", dma=True)
    smk = p.sem("p1mk", dma=True)
    swt = [p.sem(f"p1w{i}", dma=True) for i in range(2)]
    srow = p.sem("p1row", dma=True)
    syst = [p.sem(f"p1y{i}", dma=True) for i in range(2)]
    stm = p.sem("p1t")
    sv = p.sem("p1v")
    ss = p.sem("p1s")

    p.memset(sb["carry"][:], 0.0)
    p.memset(sb["ones1"][:], 1.0)
    p.memset(sb["onesT"][:], 1.0, inc=sv)
    t_init = sv.tok()
    t_mask = p.dma("sync", sb["maskc"][:], dr["maskh"], inc=smk)

    state = {
        "wt_use": [None, None],
        "ps_free": [None, None],
        "sq_free": None,
        "pssq_free": None,
        "sc_scalar": None,
        "xrs_use": None,
    }
    qkv_ready = marks["qkv"]

    def mm_sweep(base_chunk, ssq):
        ev = []
        pend_ssq = None
        for i in range(DC):
            bi = i % 2
            tlh = p.dma("sync", sb["wtile"][:, bi, 0], dr["qkvT_hi"][base_chunk + i],
                        waits=[state["wt_use"][bi]] + prep_toks(marks)
                        if i < 2 else [state["wt_use"][bi]], inc=swt[bi])
            tll = p.dma("sync", sb["wtile"][:, bi, 1], dr["qkvT_lo"][base_chunk + i],
                        inc=swt[bi])
            pb = sb[f"pA{i % 2}"]
            w0 = [(swt[bi], swt[bi].val), (sxr, sxr.val), state["ps_free"][i % 2]]
            for dc in range(DC):
                p.mm(pb[:], sb["wtile"][:, bi, 0, dc], sb["xrs_hi"][:, dc],
                     start=(dc == 0), stop=False, waits=w0 if dc == 0 else [])
                p.mm(pb[:], sb["wtile"][:, bi, 0, dc], sb["xrs_lo"][:, dc],
                     start=False, stop=False)
                p.mm(pb[:], sb["wtile"][:, bi, 1, dc], sb["xrs_hi"][:, dc],
                     start=False, stop=(dc == DC - 1),
                     inc=stm if dc == DC - 1 else None)
            t_mm = stm.tok()
            state["wt_use"][bi] = t_mm
            state["xrs_use"] = t_mm
            if pend_ssq is not None:
                pi, t_sq = pend_ssq
                t_ssq = p.mm(sb["pA3"][0:1, :], sb["ones1"][:], sb["sq512"][:],
                             start=(pi == 0), stop=(pi == DC - 1),
                             waits=[t_sq, t_init, state["pssq_free"]]
                             if pi == 0 else [t_sq], inc=stm)
                state["sq_free"] = t_ssq
            t_ev = p.copy(sb["Sbuf"][:, i], pb[:], waits=[t_mm], inc=sv)
            ev.append(t_ev)
            state["ps_free"][i % 2] = t_ev
            if ssq:
                t_sq = p.act(sb["sq512"][:], sb["Sbuf"][:, i], AF.Square,
                             waits=[t_ev, state["sq_free"]], inc=ss)
                pend_ssq = (i, t_sq)
            yield
        if ssq and pend_ssq is not None:
            pi, t_sq = pend_ssq
            t_ssq = p.mm(sb["pA3"][0:1, :], sb["ones1"][:], sb["sq512"][:],
                         start=(pi == 0), stop=(pi == DC - 1),
                         waits=[t_sq], inc=stm)
            state["sq_free"] = t_ssq
        return ev

    def rsqrt_bc(bc, rowslot, tsl):
        t_ev = p.copy(sb["sc512d"][0:1, :], sb["pA3"][0:1, :],
                      waits=[state["sq_free"], state["sc_scalar"]], inc=sv)
        state["pssq_free"] = t_ev
        p.dma("gpsimd", dr["rows"][rowslot:rowslot + 1, tsl:tsl + 512],
              sb["sc512d"][0:1, :], waits=[t_ev], inc=srow)
        t_bc = p.dma("gpsimd", sb["sc512a"][:],
                     dr["rows"][rowslot, tsl:tsl + 512].partition_broadcast(128),
                     waits=[(srow, srow.val)], inc=srow)
        t_u = p.ts(sb["sc512a"][:], sb["sc512a"][:], 1.0 / D, EPS_RMS,
                   op0=ALU.mult, op1=ALU.add, waits=[t_bc], inc=sv)
        t_s0 = p.act(sb["sc512c"][:], sb["sc512a"][:], AF.Sqrt, waits=[t_u], inc=ss)
        state["sc_scalar"] = t_s0
        p.recip(bc, sb["sc512c"][:], waits=[t_s0])
        for _ in range(2):
            p.tt(sb["sc512b"][:], bc, bc, ALU.mult)
            p.tt(sb["sc512b"][:], sb["sc512b"][:], sb["sc512a"][:], ALU.mult)
            p.ts(sb["sc512b"][:], sb["sc512b"][:], -0.5, 1.5, op0=ALU.mult, op1=ALU.add)
            p.tt(bc, bc, sb["sc512b"][:], ALU.mult)
        return p.copy(sb["sc512b"][0:1, 0:1], bc[0:1, 0:1], inc=sv)

    for sl in range(4):
        own = sl >= 2
        tsl = sl * 512
        t_xh = p.dma_t("sync", sb["xrs_hi"][:],
                       dr["xr_hi"][tsl:tsl + 512, :],
                       waits=[state["xrs_use"]] + (list(p0_toks) if sl == 0 else []),
                       inc=sxr)
        t_xl = p.dma_t("sync", sb["xrs_lo"][:], dr["xr_lo"][tsl:tsl + 512, :],
                       inc=sxr)

        for _ in mm_sweep(16, ssq=True):
            yield
        t_rk = rsqrt_bc(sb["bc0"][:, :512], 0, tsl)
        yield
        w_tok = []
        for i in range(DC):
            tb = sb["sc512c"] if i % 2 == 0 else sb["sc512d"]
            t1 = p.tt(tb[:], sb["Sbuf"][:, i], sb["bc0"][:, :512], ALU.mult,
                      waits=[w_tok[i - 2]] if i >= 2 else [t_rk], inc=sv)
            t_w = p.act(sb["wbuf"][:, i], tb[:], AF.Exp, waits=[t1], inc=ss)
            state["sc_scalar"] = t_w
            w_tok.append(t_w)
            if i % 4 == 3:
                yield

        wv_tok = []
        for i in range(DC):
            bi = i % 2
            tlh = p.dma("sync", sb["wtile"][:, bi, 0], dr["qkvT_hi"][32 + i],
                        waits=[state["wt_use"][bi]], inc=swt[bi])
            p.dma("sync", sb["wtile"][:, bi, 1], dr["qkvT_lo"][32 + i], inc=swt[bi])
            pb = sb[f"pA{i % 2}"]
            w0 = [(swt[bi], swt[bi].val), state["ps_free"][i % 2]]
            for dc in range(DC):
                p.mm(pb[:], sb["wtile"][:, bi, 0, dc], sb["xrs_hi"][:, dc],
                     start=(dc == 0), stop=False, waits=w0 if dc == 0 else [])
                p.mm(pb[:], sb["wtile"][:, bi, 0, dc], sb["xrs_lo"][:, dc],
                     start=False, stop=False)
                p.mm(pb[:], sb["wtile"][:, bi, 1, dc], sb["xrs_hi"][:, dc],
                     start=False, stop=(dc == DC - 1),
                     inc=stm if dc == DC - 1 else None)
            t_mm = stm.tok()
            state["wt_use"][bi] = t_mm
            state["xrs_use"] = t_mm
            t_wv = p.tt(sb["wvbuf"][:, i], sb["wbuf"][:, i], pb[:], ALU.mult,
                        waits=[t_mm, w_tok[i]], inc=sv)
            wv_tok.append(t_wv)
            state["ps_free"][i % 2] = t_wv
            yield

        if not own:
            for i in range(DC):
                c0, c1 = 4 * (i % 2), 4 * (i % 2) + 1
                p.red(sb["ctmp"][:, c0:c0 + 1], sb["wbuf"][:, i],
                      waits=[w_tok[i], (smk, smk.val)])
                p.tt(sb["ctmp"][:, c1:c1 + 2 - 1], sb["ctmp"][:, c0:c0 + 1],
                     sb["maskc"][:], ALU.mult)
                p.tt(sb["carry"][:, 0, i:i + 1], sb["carry"][:, 0, i:i + 1],
                     sb["ctmp"][:, c1:c1 + 1], ALU.add)
                p.red(sb["ctmp"][:, c0 + 2:c0 + 3], sb["wvbuf"][:, i],
                      waits=[wv_tok[i]])
                p.tt(sb["ctmp"][:, c1 + 2:c1 + 3], sb["ctmp"][:, c0 + 2:c0 + 3],
                     sb["maskc"][:], ALU.mult)
                p.tt(sb["carry"][:, 1, i:i + 1], sb["carry"][:, 1, i:i + 1],
                     sb["ctmp"][:, c1 + 2:c1 + 3], ALU.add)
                if i % 4 == 3:
                    yield
            continue

        for _ in mm_sweep(0, ssq=True):
            yield
        t_rq = rsqrt_bc(sb["bc1"][:, :512], 1, tsl)
        yield

        off = (sl - 2) * 512
        for i in range(DC):
            rot = i % 2
            p.scan(sb["sc512a"][:], sb["onesT"][:, :512], sb["wbuf"][:, i],
                   sb["carry"][:, 0, i:i + 1], ALU.mult, ALU.add,
                   waits=[w_tok[i]] + ([state["sc_scalar"]] if i == 0 else []))
            p.copy(sb["carry"][:, 0, i:i + 1], sb["sc512a"][:, 511:512])
            p.scan(sb["sc512b"][:], sb["onesT"][:, :512], sb["wvbuf"][:, i],
                   sb["carry"][:, 1, i:i + 1], ALU.mult, ALU.add)
            p.copy(sb["carry"][:, 1, i:i + 1], sb["sc512b"][:, 511:512])
            p.ts(sb["sc512c"][:], sb["sc512a"][:], EPS_WC, op0=ALU.add)
            p.recip(sb["sc512c"][:], sb["sc512c"][:])
            t_qs = p.tt(sb["sc512d"][:], sb["Sbuf"][:, i], sb["bc1"][:, :512], ALU.mult,
                        waits=[t_rq], inc=sv)
            t_sig = p.act(sb["sq512"][:], sb["sc512d"][:], AF.Sigmoid,
                          waits=[t_qs, state["sq_free"]], inc=ss)
            state["sc_scalar"] = t_sig
            p.tt(sb["sc512d"][:], sb["sq512"][:], sb["sc512b"][:], ALU.mult,
                 waits=[t_sig])
            p.tt(sb["sc512d"][:], sb["sc512d"][:], sb["sc512c"][:], ALU.mult)
            t_h = p.copy(sb["yst"][:, rot, 0], sb["sc512d"][:],
                         waits=[(syst[rot], syst[rot].val)], inc=sv)
            t_l = p.tt(sb["yst"][:, rot, 1], sb["sc512d"][:], sb["yst"][:, rot, 0],
                       ALU.subtract, inc=sv)
            p.dma("gpsimd", dr["y_hi"][i][:, off:off + 512], sb["yst"][:, rot, 0],
                  waits=[t_h], inc=syst[rot])
            p.dma("gpsimd", dr["y_lo"][i][:, off:off + 512], sb["yst"][:, rot, 1],
                  waits=[t_l], inc=syst[rot])
            yield

    out["p1"] = [syst[0].tok(), syst[1].tok(), sv.tok(), ss.tok(), stm.tok(),
                 (srow, srow.val)]


def emit_p4(p, sb, dr, p1_toks, gprep, marks, out):
    sy = p.sem("p4y", dma=True)
    swl = [p.sem(f"p4w{i}", dma=True) for i in range(2)]
    shs = [p.sem(f"p4h{i}", dma=True) for i in range(2)]
    stm = p.sem("p4t")
    sv = p.sem("p4v")
    ss = p.sem("p4s")

    t_yh = p.dma("sync", sb["ybuf_hi"][:], dr["y_hi"].rearrange("c p t -> p c t"),
                 waits=list(p1_toks), inc=sy)
    t_yl = p.dma("sync", sb["ybuf_lo"][:], dr["y_lo"].rearrange("c p t -> p c t"),
                 inc=sy)
    wt_use = [None, None]
    ps_free = [None, None]
    silug_use = None
    hst_use = [None, None]
    t_first = None

    for fp in range(64):
        bi = fp % 2
        ur = slice(fp * 128, (fp + 1) * 128)
        gr = slice(DFF + fp * 128, DFF + (fp + 1) * 128)
        while f"swgdone{fp // 8}" not in marks:
            next(gprep)
        tw = p.dma_t("sync", sb["swtile"][:, bi, 0, 0], dr["swN_hi"][ur, :],
                     waits=[wt_use[bi]] + prep_toks(marks), inc=swl[bi])
        p.dma_t("sync", sb["swtile"][:, bi, 1, 0], dr["swN_lo"][ur, :], inc=swl[bi])
        p.dma_t("sync", sb["swtile"][:, bi, 0, 1], dr["swN_hi"][gr, :], inc=swl[bi])
        p.dma_t("sync", sb["swtile"][:, bi, 1, 1], dr["swN_lo"][gr, :], inc=swl[bi])
        for ug in range(2):
            pb = sb["pB0"] if ug == 0 else sb["pB1"]
            w0 = [(swl[bi], swl[bi].val), (sy, sy.val), ps_free[ug]]
            for dc in range(DC):
                for th in range(2):
                    hs = slice(th * 512, (th + 1) * 512)
                    p.mm(pb[:, hs], sb["swtile"][:, bi, 0, ug, dc],
                         sb["ybuf_hi"][:, dc, hs], start=(dc == 0), stop=False,
                         waits=w0 if dc == 0 and th == 0 else [])
                    p.mm(pb[:, hs], sb["swtile"][:, bi, 0, ug, dc],
                         sb["ybuf_lo"][:, dc, hs], start=False, stop=False)
                    p.mm(pb[:, hs], sb["swtile"][:, bi, 1, ug, dc],
                         sb["ybuf_hi"][:, dc, hs], start=False, stop=(dc == DC - 1),
                         inc=stm if (dc == DC - 1 and th == 1) else None)
            if ug == 0:
                t_u = stm.tok()
            else:
                t_g = stm.tok()
        wt_use[bi] = t_g
        t_sg = p.act(sb["silug"][:], sb["pB1"][:], AF.Silu,
                     waits=[t_g, silug_use], inc=ss)
        ps_free[1] = t_sg
        t_h = p.tt(sb["ht1"][:], sb["pB0"][:], sb["silug"][:], ALU.mult,
                   waits=[t_u, t_sg], inc=sv)
        ps_free[0] = t_h
        silug_use = t_h
        rot = fp % 2
        t_hh = p.copy(sb["hst"][:, rot, 0], sb["ht1"][:],
                      waits=[(shs[rot], shs[rot].val)], inc=sv)
        t_hl = p.tt(sb["hst"][:, rot, 1], sb["ht1"][:], sb["hst"][:, rot, 0],
                    ALU.subtract, inc=sv)
        p.dma("gpsimd", dr["h_hi"][fp], sb["hst"][:, rot, 0], waits=[t_hh],
              inc=shs[rot])
        p.dma("gpsimd", dr["h_lo"][fp], sb["hst"][:, rot, 1], waits=[t_hl],
              inc=shs[rot])
        yield

    out["p4"] = [shs[0].tok(), shs[1].tok(), sv.tok(), ss.tok(), stm.tok()]


def emit_p5(p, sb, dr, p4_toks, gprep, marks, out):
    sh = [p.sem(f"p5h{i}", dma=True) for i in range(2)]
    swl = [p.sem(f"p5w{i}", dma=True) for i in range(2)]
    sxn = [p.sem(f"p5x{i}", dma=True) for i in range(2)]
    sss = [p.sem(f"p5s{i}", dma=True) for i in range(2)]
    stm = p.sem("p5t")
    sv = p.sem("p5v")

    hs_use = [None, None]
    wt_use = [None, None]
    ps_free = [None, None, None, None]

    for hsuper in range(8):
        hb = hsuper % 2
        hrows = slice(8 * hsuper, 8 * hsuper + 8)
        t_h1 = p.dma("sync", sb["h5_hi"][:, hb],
                     dr["h_hi"].rearrange("c p t -> p c t")[:, hrows, :],
                     waits=[hs_use[hb]] + (list(p4_toks) if hsuper < 2 else []),
                     inc=sh[hb])
        p.dma("sync", sb["h5_lo"][:, hb],
              dr["h_lo"].rearrange("c p t -> p c t")[:, hrows, :], inc=sh[hb])
        while f"wojdone{hsuper // 2}" not in marks:
            next(gprep)
        for dout in range(DC):
            wb = dout % 2
            rows = slice(dout * 128, (dout + 1) * 128)
            cols = slice(hsuper * 1024, (hsuper + 1) * 1024)
            p.dma_t("sync", sb["wotile"][:, wb, 0], dr["woN_hi"][rows, cols],
                    waits=[wt_use[wb]] +
                    (prep_toks(marks) if dout < 2 else []), inc=swl[wb])
            p.dma_t("sync", sb["wotile"][:, wb, 1], dr["woN_lo"][rows, cols],
                    inc=swl[wb])
            for tb in range(2):
                bank = (dout % 2) * 2 + tb
                pb = sb[f"pA{bank}"]
                ts_ = slice(tb * 512, (tb + 1) * 512)
                w0 = [(swl[wb], swl[wb].val), (sh[hb], sh[hb].val), ps_free[bank]]
                for hc in range(8):
                    p.mm(pb[:], sb["wotile"][:, wb, 0, hc],
                         sb["h5_hi"][:, hb, hc, ts_], start=(hc == 0), stop=False,
                         waits=w0 if hc == 0 else [])
                    p.mm(pb[:], sb["wotile"][:, wb, 0, hc],
                         sb["h5_lo"][:, hb, hc, ts_], start=False, stop=False)
                    p.mm(pb[:], sb["wotile"][:, wb, 1, hc],
                         sb["h5_hi"][:, hb, hc, ts_], start=False, stop=(hc == 7),
                         inc=stm if hc == 7 else None)
                t_mm = stm.tok()
                wt_use[wb] = t_mm
                hs_use[hb] = t_mm
                if hsuper == 0:
                    t_acc = p.copy(sb["oacc"][:, dout, ts_], pb[:],
                                   waits=[t_mm], inc=sv)
                else:
                    t_acc = p.tt(sb["oacc"][:, dout, ts_], sb["oacc"][:, dout, ts_],
                                 pb[:], ALU.add, waits=[t_mm], inc=sv)
                ps_free[bank] = t_acc
            yield

    xn_use = [None, None]
    sst_use = [None, None]
    for dout in range(DC):
        bi = dout % 2
        p.dma_t("sync", sb["xn5h"][:, bi], dr["xn_hi"][:, dout * 128:(dout + 1) * 128],
                waits=[xn_use[bi]], inc=sxn[bi])
        p.dma_t("sync", sb["xn5l"][:, bi], dr["xn_lo"][:, dout * 128:(dout + 1) * 128],
                inc=sxn[bi])
        t_1 = p.tt(sb["sfin"][:, bi], sb["xn5h"][:, bi], sb["xn5l"][:, bi], ALU.add,
                   waits=[(sxn[bi], sxn[bi].val), (sss[bi], sss[bi].val)], inc=sv)
        xn_use[bi] = t_1
        t_s = p.tt(sb["sfin"][:, bi], sb["sfin"][:, bi], sb["oacc"][:, dout, :],
                   ALU.add, inc=sv)
        p.dma("gpsimd", dr["s_fm"][dout], sb["sfin"][:, bi], waits=[t_s], inc=sss[bi])
        yield

    out["p5"] = [sss[0].tok(), sss[1].tok(), sv.tok(), stm.tok(),
                 (sxn[0], sxn[0].val), (sxn[1], sxn[1].val)]


def emit_p3(p, sb, dr, prev_toks, gprep, marks, out):
    sxn = p.sem("p3xn", dma=True)
    swr = p.sem("p3wr", dma=True)
    sgu = [p.sem(f"p3g{i}", dma=True) for i in range(2)]
    sdt = p.sem("p3dt", dma=True)
    sce = p.sem("p3ce", dma=True)
    scf = p.sem("p3cf", dma=True)
    sma = [p.sem(f"p3m{i}", dma=True) for i in range(2)]
    sms = [p.sem(f"p3n{i}", dma=True) for i in range(2)]
    stm = p.sem("p3t")
    sv = p.sem("p3v")
    ss = p.sem("p3s")

    t_x1 = p.dma_t("sync", sb["xnf3h"][:], dr["xn_hi"],
                   waits=list(prev_toks), inc=sxn)
    t_x2 = p.dma_t("sync", sb["xnf3l"][:], dr["xn_lo"], inc=sxn)
    t_wr = p.dma("sync", sb["wrs"][:], dr["w_r"].rearrange("(c p) e -> p c e", p=128),
                 inc=swr)
    t_eye = p.dma("sync", sb["eyes"][:], dr["eye128"], inc=swr)
    t_io = p.dma("sync", sb["iotas"][:], dr["iota16"], inc=swr)
    t_wrh = p.act(sb["wrh"][:], sb["wrs"][:], AF.Identity,
                  waits=[(swr, swr.val)], inc=ss)
    t_wrl = p.tt(sb["wrl"][:], sb["wrs"][:], sb["wrh"][:], ALU.subtract,
                 waits=[t_wrh], inc=sv)
    yield

    ctile_use = None
    pA3_use = None
    for tt in range(8):
        pb = sb["pA2"]
        w0 = [(sxn, sxn.val), t_wrl, ctile_use]
        for dc in range(DC):
            l_hi = sb["xnf3h"][:, dc, tt * 128:(tt + 1) * 128]
            l_lo = sb["xnf3l"][:, dc, tt * 128:(tt + 1) * 128]
            p.mm(pb[:, :NE], l_hi, sb["wrh"][:, dc], start=(dc == 0), stop=False,
                 waits=w0 if dc == 0 else [])
            p.mm(pb[:, :NE], l_lo, sb["wrh"][:, dc], start=False, stop=False)
            p.mm(pb[:, :NE], l_hi, sb["wrl"][:, dc], start=False, stop=(dc == DC - 1),
                 inc=stm if dc == DC - 1 else None)
        t_mm = stm.tok()
        q = sb["rtq"]
        sc = q[:, 0:16]
        t_sc = p.copy(sc, pb[:, :NE], waits=[t_mm], inc=sv)
        p.red(q[:, 16:17], sc, alu=ALU.max)
        p.ts(q[:, 17:18], q[:, 16:17], -1.0)
        t_sub = p.ts(q[:, 20:36], sc, q[:, 17:18], op0=ALU.add, inc=sv)
        t_ex = p.act(q[:, 20:36], q[:, 20:36], AF.Exp, waits=[t_sub], inc=ss)
        t_sm = p.red(q[:, 18:19], q[:, 20:36], waits=[t_ex], inc=sv)
        p.recip(q[:, 19:20], q[:, 18:19])
        p.ts(q[:, 20:36], q[:, 20:36], q[:, 19:20], op0=ALU.mult)
        p.red(q[:, 36:37], q[:, 20:36], alu=ALU.max)
        p.ts(q[:, 40:56], q[:, 20:36], q[:, 36:37], op0=ALU.is_ge)
        p.ts(q[:, 40:56], q[:, 40:56], -1e6, 1e6, op0=ALU.mult, op1=ALU.add)
        p.tt(q[:, 40:56], q[:, 40:56], sb["iotas"][:], ALU.add)
        p.red(q[:, 37:38], q[:, 40:56], alu=ALU.min)
        p.ts(q[:, 40:56], sb["iotas"][:], q[:, 37:38], op0=ALU.is_equal)
        p.tt(q[:, 56:72], q[:, 40:56], q[:, 20:36], ALU.mult)
        p.tt(q[:, 56:72], q[:, 20:36], q[:, 56:72], ALU.subtract)
        p.red(q[:, 38:39], q[:, 56:72], alu=ALU.max)
        p.tt(q[:, 39:40], q[:, 36:37], q[:, 38:39], ALU.add)
        p.recip(q[:, 39:40], q[:, 39:40])
        p.ts(q[:, 56:72], q[:, 56:72], q[:, 38:39], op0=ALU.is_ge)
        p.ts(q[:, 56:72], q[:, 56:72], -1e6, 1e6, op0=ALU.mult, op1=ALU.add)
        p.tt(q[:, 56:72], q[:, 56:72], sb["iotas"][:], ALU.add)
        p.red(q[:, 37:38], q[:, 56:72], alu=ALU.min)
        p.ts(q[:, 56:72], sb["iotas"][:], q[:, 37:38], op0=ALU.is_equal)
        p.tt(q[:, 72:73], q[:, 36:37], q[:, 39:40], ALU.mult)
        p.tt(q[:, 73:74], q[:, 38:39], q[:, 39:40], ALU.mult)
        p.ts(q[:, 40:56], q[:, 40:56], q[:, 72:73], op0=ALU.mult)
        p.ts(q[:, 56:72], q[:, 56:72], q[:, 73:74], op0=ALU.mult)
        t_c = p.tt(q[:, 40:56], q[:, 40:56], q[:, 56:72], ALU.add, inc=sv)
        t_tr = p.op("tensor",
                    lambda eh, o=sb["pA3"][0:NE, 0:128], i=q[:, 40:56],
                    e=sb["eyes"][:]: eh.transpose(o, i, e),
                    waits=[t_c, (swr, swr.val), pA3_use], inc=stm)
        t_cf = p.copy(sb["cfm"][0:NE, tt * 128:(tt + 1) * 128],
                      sb["pA3"][0:NE, 0:128], waits=[t_tr], inc=sv)
        ctile_use = t_cf
        pA3_use = t_cf
        last_sc = t_sc
        yield
    p.dma("gpsimd", dr["c_fm"], sb["cfm"][0:NE, :], waits=[ctile_use], inc=scf)

    gu_par_use = [None, None]
    gus_slot = [None, None]
    dt_mm_use = None
    dts_split_use = None
    ps_freeA = [last_sc, last_sc, last_sc, pA3_use]
    ps_freeB = [None, None]
    ce_use = None
    htil_use = None
    macc_use = [None, None]
    silug_use = [None, None]
    chain_rot = 0
    fch_ctr = 0
    for e in range(NE):
        while f"moe{e}done" not in marks:
            next(gprep)
        t_t2 = None
        t_ce = p.dma("gpsimd", sb["cebc"][:],
                     dr["c_fm"][e, :].partition_broadcast(128),
                     waits=[ce_use, (scf, scf.val)], inc=sce)
        for fch in range(3):
            par = fch_ctr % 2
            fch_ctr += 1
            cs = slice(fch * 256, (fch + 1) * 256)
            p.dma("sync", sb["guh"][:, par, 0],
                  dr["gateN_hi"][e][:, cs].rearrange("(c p) f -> p c f", p=128),
                  waits=[gu_par_use[par]] + prep_toks(marks), inc=sgu[par])
            p.dma("sync", sb["gul"][:, par, 0],
                  dr["gateN_lo"][e][:, cs].rearrange("(c p) f -> p c f", p=128),
                  inc=sgu[par])
            p.dma("sync", sb["guh"][:, par, 1],
                  dr["upN_hi"][e][:, cs].rearrange("(c p) f -> p c f", p=128),
                  inc=sgu[par])
            p.dma("sync", sb["gul"][:, par, 1],
                  dr["upN_lo"][e][:, cs].rearrange("(c p) f -> p c f", p=128),
                  inc=sgu[par])
            t_split = (sgu[par], sgu[par].val)
            for fc3 in range(2):
              fc = fch * 2 + fc3
              fs = slice(fc3 * 128, (fc3 + 1) * 128)
              for th in range(2):
                hs = slice(th * 512, (th + 1) * 512)
                pg = sb[f"pA{2 * th}"]
                pu = sb[f"pA{2 * th + 1}"]
                w0 = [t_split, (sxn, sxn.val), ps_freeA[2 * th],
                      ps_freeA[2 * th + 1]]
                for dc in range(DC):
                    p.mm(pg[:], sb["guh"][:, par, 0, dc, fs], sb["xnf3h"][:, dc, hs],
                         start=(dc == 0), stop=False, waits=w0 if dc == 0 else [])
                    p.mm(pg[:], sb["guh"][:, par, 0, dc, fs], sb["xnf3l"][:, dc, hs],
                         start=False, stop=False)
                    p.mm(pg[:], sb["gul"][:, par, 0, dc, fs], sb["xnf3h"][:, dc, hs],
                         start=False, stop=(dc == DC - 1))
                    p.mm(pu[:], sb["guh"][:, par, 1, dc, fs], sb["xnf3h"][:, dc, hs],
                         start=(dc == 0), stop=False)
                    p.mm(pu[:], sb["guh"][:, par, 1, dc, fs], sb["xnf3l"][:, dc, hs],
                         start=False, stop=False)
                    p.mm(pu[:], sb["gul"][:, par, 1, dc, fs], sb["xnf3h"][:, dc, hs],
                         start=False, stop=(dc == DC - 1),
                         inc=stm if dc == DC - 1 else None)
                t_mm = stm.tok()
                gu_par_use[par] = t_mm
                cr = chain_rot % 2
                chain_rot += 1
                csl = slice(cr * 512, (cr + 1) * 512)
                t_sg = p.act(sb["silug"][:, csl], pg[:], AF.Silu,
                             waits=[t_mm, silug_use[cr]], inc=ss)
                t_t1 = p.tt(sb["mt1"][:, csl], pu[:], sb["silug"][:, csl],
                            ALU.mult, waits=[t_sg, (sce, sce.val)], inc=sv)
                ps_freeA[2 * th] = t_t1
                ps_freeA[2 * th + 1] = t_t1
                silug_use[cr] = t_t1
                t_t2 = p.tt(sb["mt1"][:, csl], sb["mt1"][:, csl],
                            sb["cebc"][:, hs], ALU.mult, inc=sv)
                t_hh = p.copy(sb["htil_hi"][:, fc, hs], sb["mt1"][:, csl],
                              waits=[htil_use] if (fch, fc3, th) == (0, 0, 0) else [],
                              inc=sv)
                p.tt(sb["htil_lo"][:, fc, hs], sb["mt1"][:, csl],
                     sb["htil_hi"][:, fc, hs], ALU.subtract, inc=sv)
            yield
        ce_use = t_t2
        t_hready = sv.tok()

        for dgc in range(DC):
            p.dma("sync", sb["dth"][:],
                  dr["downN_hi"][e].rearrange("(c p) d -> p c d", p=128)
                  [:, :, dgc * 128:(dgc + 1) * 128],
                  waits=[dt_mm_use] + (prep_toks(marks) if dgc == 0 else []),
                  inc=sdt)
            p.dma("sync", sb["dtl"][:],
                  dr["downN_lo"][e].rearrange("(c p) d -> p c d", p=128)
                  [:, :, dgc * 128:(dgc + 1) * 128], inc=sdt)
            tl_d = (sdt, sdt.val)
            pb = sb[f"pB{dgc % 2}"]
            mrot = dgc % 2
            if e > 0:
                p.dma("gpsimd", sb["macc"][:, mrot], dr["moe_fm"][dgc],
                      waits=[macc_use[mrot], (sms[mrot], sms[mrot].val)],
                      inc=sma[mrot])
            w0 = [tl_d, t_hready, ps_freeB[dgc % 2]]
            for fc in range(6):
                for th in range(2):
                    hs = slice(th * 512, (th + 1) * 512)
                    p.mm(pb[:, hs], sb["dth"][:, fc], sb["htil_hi"][:, fc, hs],
                         start=(fc == 0), stop=False, waits=w0 if fc == 0 else [])
                    p.mm(pb[:, hs], sb["dth"][:, fc], sb["htil_lo"][:, fc, hs],
                         start=False, stop=False)
                    p.mm(pb[:, hs], sb["dtl"][:, fc], sb["htil_hi"][:, fc, hs],
                         start=False, stop=(fc == 5),
                         inc=stm if (fc, th) == (5, 1) else None)
            t_mm = stm.tok()
            dt_mm_use = t_mm
            htil_use = t_mm
            if e == 0:
                t_acc = p.copy(sb["macc"][:, mrot], pb[:],
                               waits=[t_mm, (sms[mrot], sms[mrot].val),
                                      macc_use[mrot]], inc=sv)
            else:
                t_acc = p.tt(sb["macc"][:, mrot], sb["macc"][:, mrot], pb[:],
                             ALU.add, waits=[t_mm, (sma[mrot], sma[mrot].val)],
                             inc=sv)
            ps_freeB[dgc % 2] = t_acc
            macc_use[mrot] = t_acc
            p.dma("gpsimd", dr["moe_fm"][dgc], sb["macc"][:, mrot],
                  waits=[t_acc], inc=sms[mrot])
            yield

    out["p3"] = [sms[0].tok(), sms[1].tok(), sv.tok(), ss.tok(), stm.tok(),
                 (sce, sce.val), (scf, scf.val), (sxn, sxn.val)]


def emit_p6(p, sb, dr, prev_toks, out):
    sl6 = [p.sem(f"p6l{i}", dma=True) for i in range(2)]
    sbc = p.sem("p6bc", dma=True)
    sos = [p.sem(f"p6o{i}", dma=True) for i in range(2)]
    sv = p.sem("p6v")
    ss = p.sem("p6s")

    p.dma("gpsimd", sb["mbc"][:], dr["mrow"][:, 0].partition_broadcast(128),
          waits=list(prev_toks), inc=sbc)
    p.dma("gpsimd", sb["stdbc"][:], dr["stdrow"][:, 0].partition_broadcast(128),
          inc=sbc)
    buf_use = [None, None]
    g6_use = None
    for dc in range(DC):
        bi = dc % 2
        p.dma("sync", sb["s6"][:, bi], dr["s_fm"][dc],
              waits=[buf_use[bi]] + (list(prev_toks) if dc < 2 else []),
              inc=sl6[bi])
        p.dma("sync", sb["m6"][:, bi], dr["moe_fm"][dc], inc=sl6[bi])
        p.dma_t("sync", sb["xn6h"][:, bi], dr["xn_hi"][:, dc * 128:(dc + 1) * 128],
                inc=sl6[bi])
        p.dma_t("sync", sb["xn6l"][:, bi], dr["xn_lo"][:, dc * 128:(dc + 1) * 128],
                inc=sl6[bi])
        t_g = p.act(sb["g6"][:], sb["m6"][:, bi], AF.Silu,
                    waits=[(sl6[bi], sl6[bi].val), g6_use], inc=ss)
        t_1 = p.tt(sb["t6"][:], sb["xn6h"][:, bi], sb["xn6l"][:, bi], ALU.add,
                   waits=[(sl6[bi], sl6[bi].val), (sbc, sbc.val)], inc=sv)
        p.tt(sb["t6"][:], sb["t6"][:], sb["stdbc"][:], ALU.mult)
        p.tt(sb["t6"][:], sb["t6"][:], sb["mbc"][:], ALU.add)
        t_4 = p.tt(sb["u6"][:], sb["s6"][:, bi], sb["g6"][:], ALU.mult,
                   waits=[t_g], inc=sv)
        g6_use = t_4
        t_o = p.tt(sb["o6"][:, bi], sb["t6"][:], sb["u6"][:], ALU.add,
                   waits=[(sos[bi], sos[bi].val)], inc=sv)
        buf_use[bi] = t_o
        p.dma("gpsimd", dr["out_fm"][dc], sb["o6"][:, bi], waits=[t_o], inc=sos[bi])
        yield

    out["p6"] = [sos[0].tok(), sos[1].tok()]



def build_nc(dev=False):
    nc = bass.Bass("TRN2", target_bir_lowering=False, detect_race_conditions=False)
    p = Prog()

    dr = {}
    dr["x_ext"] = nc.dram_tensor("x_ext", [TEXT, D], F32, kind="ExternalInput").ap()
    dr["w_qkv"] = nc.dram_tensor("w_qkv", [3 * D, D], F32, kind="ExternalInput").ap()
    dr["w_sw"] = nc.dram_tensor("w_sw", [2 * DFF, D], F32, kind="ExternalInput").ap()
    dr["w_out"] = nc.dram_tensor("w_out", [D, DFF], F32, kind="ExternalInput").ap()
    dr["w_r"] = nc.dram_tensor("w_r", [D, NE], F32, kind="ExternalInput").ap()
    dr["w_gate"] = nc.dram_tensor("w_gate", [NE, D, FMOE], F32, kind="ExternalInput").ap()
    dr["w_up"] = nc.dram_tensor("w_up", [NE, D, FMOE], F32, kind="ExternalInput").ap()
    dr["w_down"] = nc.dram_tensor("w_down", [NE, FMOE, D], F32, kind="ExternalInput").ap()
    dr["maskh"] = nc.dram_tensor("maskh", [128, 1], F32, kind="ExternalInput").ap()
    dr["iota16"] = nc.dram_tensor("iota16", [128, NE], F32, kind="ExternalInput").ap()
    dr["eye128"] = nc.dram_tensor("eye128", [128, 128], F32, kind="ExternalInput").ap()

    dr["out_fm"] = nc.dram_tensor("out_fm", [DC, 128, TOK], F32, kind="ExternalOutput").ap()

    def scr(name, shape, dt):
        if name in DEV_OUT:
            dr[name] = nc.dram_tensor(name, shape, dt, kind="ExternalOutput").ap()
        else:
            dr[name] = nc.dram_tensor(name, shape, dt).ap()
        return dr[name]

    scr("xr_hi", [TEXT, D], BF16)
    scr("xr_lo", [TEXT, D], BF16)
    scr("xn_hi", [TOK, D], BF16)
    scr("xn_lo", [TOK, D], BF16)
    scr("mrow", [TOK, 1], F32)
    scr("stdrow", [TOK, 1], F32)
    scr("rows", [8, TEXT], F32)
    scr("qkvT_hi", [48, 128, DC, 128], BF16)
    scr("qkvT_lo", [48, 128, DC, 128], BF16)
    scr("swN_hi", [2 * DFF, D], BF16)
    scr("swN_lo", [2 * DFF, D], BF16)
    scr("woN_hi", [D, DFF], BF16)
    scr("woN_lo", [D, DFF], BF16)
    scr("gateN_hi", [NE, D, FMOE], BF16)
    scr("gateN_lo", [NE, D, FMOE], BF16)
    scr("upN_hi", [NE, D, FMOE], BF16)
    scr("upN_lo", [NE, D, FMOE], BF16)
    scr("downN_hi", [NE, FMOE, D], BF16)
    scr("downN_lo", [NE, FMOE, D], BF16)
    scr("y_hi", [DC, 128, TOK], BF16)
    scr("y_lo", [DC, 128, TOK], BF16)
    scr("h_hi", [64, 128, TOK], BF16)
    scr("h_lo", [64, 128, TOK], BF16)
    scr("s_fm", [DC, 128, TOK], F32)
    scr("moe_fm", [DC, 128, TOK], F32)
    scr("c_fm", [NE, TOK], F32)
    scr("dbg", [16, 128, 24], F32)

    with contextlib.ExitStack() as es:
        sb = {}
        ARENA = 204 * 1024
        arena_base = (nc.sbuf_base + 31) // 32 * 32
        nc.alloc_sbuf_tensor("arena", [128, ARENA + 64], mybir.dt.uint8)

        def place(name, shape, dt, off):
            assert off % 32 == 0, name
            t = nc.alloc_sbuf_tensor_at(name, shape, dt, offset=arena_base + off)
            nbytes = int(np.prod(shape[1:])) * (2 if dt == BF16 else 4)
            assert off + nbytes <= ARENA, (name, off, nbytes)
            sb[name] = t
            return t

        G = 0
        PREP = 512
        R = 25088

        place("ones1", [128, 1], F32, G + 0)
        place("rowst", [128, 16], F32, G + 32)
        place("carry", [128, 2, DC], F32, G + 96)
        place("ctmp", [128, 8], F32, G + 224)
        place("maskc", [128, 1], F32, G + 288)

        place("wslab", [128, 4, 1024], F32, PREP + 0)
        place("wshi", [128, 2, 1024], BF16, PREP + 16384)
        place("wslo", [128, 2, 1024], BF16, PREP + 20480)
        place("xbout", [128, 2, 8, 128], BF16, R + 165888)

        place("xt", [128, 2, D], F32, R + 0)
        place("tmpa", [128, D], F32, R + 16384)
        place("tmpb", [128, D], F32, R + 24576)
        place("hsta", [128, 2, 2, D], BF16, R + 32768)
        place("lsta", [128, 2, 2, D], BF16, R + 49152)
        place("stats", [128, 48], F32, R + 65536)
        place("xrs_hi", [128, DC, 512], BF16, R + 0)
        place("xrs_lo", [128, DC, 512], BF16, R + 16384)
        place("wtile", [128, 2, 2, DC, 128], BF16, R + 32768)
        place("Sbuf", [128, DC, 512], F32, R + 49152)
        place("wbuf", [128, DC, 512], F32, R + 81920)
        place("wvbuf", [128, DC, 512], F32, R + 114688)
        place("sq512", [128, 512], F32, R + 147456)
        place("bc0", [128, 512], F32, R + 149504)
        place("bc1", [128, 512], F32, R + 153600)
        place("onesT", [128, TOK], F32, R + 157696)
        place("sc512a", [128, 512], F32, R + 161792)
        place("sc512b", [128, 512], F32, R + 163840)
        place("sc512c", [128, 512], F32, R + 165888)
        place("sc512d", [128, 512], F32, R + 167936)
        place("yst", [128, 2, 2, 512], BF16, R + 169984)
        place("ybuf_hi", [128, DC, 1024], BF16, R + 0)
        place("ybuf_lo", [128, DC, 1024], BF16, R + 32768)
        place("swtile", [128, 2, 2, 2, DC, 128], BF16, R + 65536)
        place("silug", [128, 1024], F32, R + 98304)
        place("ht1", [128, 1024], F32, R + 102400)
        place("hst", [128, 2, 2, 1024], BF16, R + 106496)
        place("oacc", [128, DC, 1024], F32, R + 0)
        place("h5_hi", [128, 2, 8, 1024], BF16, R + 65536)
        place("h5_lo", [128, 2, 8, 1024], BF16, R + 98304)
        place("wotile", [128, 2, 2, 8, 128], BF16, R + 131072)
        place("xn5h", [128, 2, 1024], BF16, R + 139264)
        place("xn5l", [128, 2, 1024], BF16, R + 143360)
        place("sfin", [128, 2, 1024], F32, R + 147456)
        place("xnf3h", [128, DC, 1024], BF16, R + 0)
        place("xnf3l", [128, DC, 1024], BF16, R + 32768)
        place("htil_hi", [128, 6, 1024], BF16, R + 65536)
        place("htil_lo", [128, 6, 1024], BF16, R + 77824)
        place("guh", [128, 2, 2, DC, 256], BF16, R + 90112)
        place("gul", [128, 2, 2, DC, 256], BF16, R + 122880)
        place("silug", [128, 1024], F32, R + 155648)
        place("mt1", [128, 1024], F32, R + 159744)
        place("cebc", [128, 1024], F32, R + 163840)
        place("macc", [128, 2, 1024], F32, R + 167936)
        place("dth", [128, 6, 128], BF16, R + 176128)
        place("dtl", [128, 6, 128], BF16, R + 177664)
        place("wrs", [128, DC, NE], F32, R + 179200)
        place("wrh", [128, DC, NE], BF16, R + 180224)
        place("wrl", [128, DC, NE], BF16, R + 180736)
        place("eyes", [128, 128], F32, R + 181248)
        place("iotas", [128, NE], F32, R + 181760)
        place("rtq", [128, 80], F32, R + 181888)
        place("cfm", [128, 1024], F32, R + 167936)
        place("s6", [128, 2, 1024], F32, R + 0)
        place("m6", [128, 2, 1024], F32, R + 8192)
        place("xn6h", [128, 2, 1024], BF16, R + 16384)
        place("xn6l", [128, 2, 1024], BF16, R + 20480)
        place("mbc", [128, 1024], F32, R + 24576)
        place("stdbc", [128, 1024], F32, R + 28672)
        place("g6", [128, 1024], F32, R + 32768)
        place("t6", [128, 1024], F32, R + 36864)
        place("u6", [128, 1024], F32, R + 40960)
        place("o6", [128, 2, 1024], F32, R + 45056)

        def psum(name, shape):
            sb[name] = es.enter_context(nc.psum_tensor(name, shape, F32))
        for i in range(4):
            psum(f"pA{i}", [128, 512])
        for i in range(2):
            psum(f"pB{i}", [128, 1024])

        marks, out = {}, {}
        gprep = emit_prep(p, sb, dr, marks)
        for _ in emit_p0(p, sb, dr, out):
            for _ in range(4):
                next(gprep, None)
        for _ in emit_p1(p, sb, dr, out["p0"], marks, out):
            pass
        for _ in emit_p3(p, sb, dr, out["p1"], gprep, marks, out):
            for _ in range(3):
                next(gprep, None)
        for _ in emit_p4(p, sb, dr, out["p3"], gprep, marks, out):
            next(gprep, None)
        for _ in emit_p5(p, sb, dr, out["p4"], gprep, marks, out):
            next(gprep, None)
        for _ in gprep:
            pass
        for _ in emit_p6(p, sb, dr, list(out["p5"]) + list(out["p3"]), out):
            pass
        p.wait("gpsimd", *out["p6"])

        for s in p.sems:
            s.h = es.enter_context(nc.semaphore(s.name))
        block = es.enter_context(nc.Block())

        for eng in ENGINES:
            def emit(eh, eng=eng):
                for f in p.q[eng]:
                    f(eh)
            getattr(block, eng)(emit)

    return nc



def kernel(L0_attn_x, L0_attn_w_qkv, L0_attn_w_swiglu, L0_attn_w_out,
           L0_ffn_router_w, L0_ffn_gate_w, L0_ffn_up_w, L0_ffn_down_w):
    x = np.ascontiguousarray(np.asarray(L0_attn_x, dtype=np.float32))
    wq = np.ascontiguousarray(np.asarray(L0_attn_w_qkv, dtype=np.float32))
    wsw = np.ascontiguousarray(np.asarray(L0_attn_w_swiglu, dtype=np.float32))
    wo = np.ascontiguousarray(np.asarray(L0_attn_w_out, dtype=np.float32))
    wr = np.ascontiguousarray(np.asarray(L0_ffn_router_w, dtype=np.float32))
    wg = np.ascontiguousarray(np.asarray(L0_ffn_gate_w, dtype=np.float32))
    wu = np.ascontiguousarray(np.asarray(L0_ffn_up_w, dtype=np.float32))
    wd = np.ascontiguousarray(np.asarray(L0_ffn_down_w, dtype=np.float32))

    iota16 = np.tile(np.arange(NE, dtype=np.float32), (128, 1))
    eye128 = np.eye(128, dtype=np.float32)

    nc = build_nc()
    in_maps = []
    for c in range(8):
        b, h = c // 2, c % 2
        own = x[b, h * TOK:(h + 1) * TOK]
        prev = x[b, 0:TOK] if h == 1 else own
        x_ext = np.concatenate([prev, own], axis=0)
        in_maps.append(dict(
            x_ext=x_ext, w_qkv=wq, w_sw=wsw, w_out=wo, w_r=wr,
            w_gate=wg, w_up=wu, w_down=wd,
            maskh=np.full((128, 1), float(h), np.float32),
            iota16=iota16, eye128=eye128,
        ))
    res = run_bass_kernel_spmd(nc, in_maps, core_ids=list(range(8)))
    out = np.empty((B, S, D), np.float32)
    for c in range(8):
        b, h = c // 2, c % 2
        o = np.asarray(res.results[c]["out_fm"])
        out[b, h * TOK:(h + 1) * TOK, :] = o.reshape(D, TOK).T
    return out


# version 08
# speedup vs baseline: 1.3039x; 1.3039x over previous
import contextlib
import numpy as np
import ml_dtypes
import concourse.bass as bass
import concourse.mybir as mybir
from concourse.bass_utils import run_bass_kernel_spmd

F32 = mybir.dt.float32
BF16 = mybir.dt.bfloat16
AF = mybir.ActivationFunctionType
ALU = mybir.AluOpType
AX = mybir.AxisListType

B, S, D = 4, 2048, 2048
TOK = 1024
TEXT = 2048
DC = D // 128
DFF = 8192
NE, FMOE, KTOP = 16, 768, 2
EPS_RMS = 1.1920929e-07
EPS_LN = 1e-5
EPS_WC = 1e-6

ENGINES = ("sync", "scalar", "vector", "tensor", "gpsimd")
DEV_OUT = ()


class Sem:
    def __init__(self, name, dma=False):
        self.name = name
        self.dma = dma
        self.val = 0
        self.h = None

    def tok(self):
        return (self, self.val)


class Prog:

    def __init__(self):
        self.q = {e: [] for e in ENGINES}
        self.sems = []
        self.pending = {e: set() for e in ENGINES}
        self.last_w = {e: {} for e in ENGINES}
        self.hz = None

    def sem(self, name, dma=False):
        s = Sem(name, dma)
        self.sems.append(s)
        return s

    @staticmethod
    def _name(x):
        t = getattr(x, "tensor", None)
        if t is not None:
            return getattr(t, "name", None)
        if hasattr(x, "shape") and hasattr(x, "name"):
            return x.name
        return None

    def _names(self, xs):
        out = set()
        for x in xs:
            n = self._name(x)
            if n is not None:
                out.add(n)
        return out

    def op(self, eng, fn, waits=(), inc=None, inc_n=1, rd=(), wr=()):
        waits = list(w for w in waits if w is not None)
        rdn = self._names(rd)
        wrn = self._names(wr)
        if eng == "vector" and rdn & self.pending[eng]:
            self.q[eng].append(lambda eh: eh.drain())
            self.pending[eng] = set()
        elif eng == "scalar":
            hazard = rdn & self.pending[eng]
            for n in hazard:
                t = self.last_w[eng].get(n)
                if t is not None:
                    waits.append(t)
                self.pending[eng].discard(n)
            if wrn and inc is None:
                if self.hz is None:
                    self.hz = self.sem("hzs")
                inc = self.hz
                inc_n = 1
        for s, v in waits:
            assert not s.dma or v == s.val, (
                f"partial wait on dma sem {s.name}: {v} < {s.val}")
        if inc is not None:
            inc.val += inc_n

        def run(eh, fn=fn, waits=waits, inc=inc, inc_n=inc_n):
            for s, v in waits:
                if v > 0:
                    eh.wait_ge(s.h, v)
            instr = fn(eh) if fn is not None else None
            if inc is not None:
                assert instr is not None, "inc requires an instruction"
                instr.then_inc(inc.h, inc_n)
            return instr

        self.q[eng].append(run)
        self.pending[eng] |= wrn
        if eng == "scalar" and inc is not None:
            for n in wrn:
                self.last_w[eng][n] = inc.tok()
        return inc.tok() if inc is not None else None

    def dma(self, eng, out, in_, waits=(), inc=None):
        return self.op(eng, lambda eh, o=out, i=in_: eh.dma_start(out=o, in_=i),
                       waits=waits, inc=inc, inc_n=16)

    def dma_t(self, eng, out, in_, waits=(), inc=None):
        return self.op(eng, lambda eh, o=out, i=in_: eh.dma_start_transpose(out=o, in_=i),
                       waits=waits, inc=inc, inc_n=16)

    def tt(self, out, a, b, alu, waits=(), inc=None, eng="vector"):
        return self.op(eng, lambda eh, o=out, x=a, y=b, u=alu:
                       eh.tensor_tensor(o, x, y, op=u), waits=waits, inc=inc,
                       rd=[a, b], wr=[out])

    def ts(self, out, in0, s1, s2=None, op0=ALU.mult, op1=None, waits=(), inc=None,
           eng="vector", accum=None):
        kw = {}
        if op1 is not None:
            kw["op1"] = op1
        if accum is not None:
            kw["accum_out"] = accum
        return self.op(eng, lambda eh, o=out, i=in0, a=s1, b=s2, u=op0, k=kw:
                       eh.tensor_scalar(out=o, in0=i, scalar1=a, scalar2=b, op0=u, **k),
                       waits=waits, inc=inc, rd=[in0, s1, s2], wr=[out, accum])

    def copy(self, out, in_, waits=(), inc=None, eng="vector"):
        return self.op(eng, lambda eh, o=out, i=in_: eh.tensor_copy(o, i),
                       waits=waits, inc=inc, rd=[in_], wr=[out])

    def recip(self, out, in_, waits=(), inc=None):
        return self.op("vector", lambda eh, o=out, i=in_: eh.reciprocal(o, i),
                       waits=waits, inc=inc, rd=[in_], wr=[out])

    def red(self, out, in_, alu=ALU.add, waits=(), inc=None, negate=None):
        if alu == ALU.add:
            return self.op("vector", lambda eh, o=out, i=in_:
                           eh.reduce_sum(o, i, axis=AX.X), waits=waits, inc=inc,
                           rd=[in_], wr=[out])
        return self.op("vector", lambda eh, o=out, i=in_, u=alu, n=negate:
                       eh.tensor_reduce(o, i, axis=AX.X, op=u, negate=n),
                       waits=waits, inc=inc, rd=[in_], wr=[out])

    def scan(self, out, data0, data1, initial, op0, op1, waits=(), inc=None):
        return self.op("vector", lambda eh, o=out, d0=data0, d1=data1, i=initial,
                       u0=op0, u1=op1:
                       eh.tensor_tensor_scan(o, d0, d1, i, op0=u0, op1=u1),
                       waits=waits, inc=inc, rd=[data0, data1, initial], wr=[out])

    def memset(self, ap, val, waits=(), inc=None, eng="vector"):
        return self.op(eng, lambda eh, a=ap, v=val: eh.memset(a, v),
                       waits=waits, inc=inc, wr=[ap])

    def act(self, out, in_, func, bias=None, scale=None, accum=None, waits=(), inc=None):
        kw = {}
        if bias is not None:
            kw["bias"] = bias
        if scale is not None:
            kw["scale"] = scale
        if accum is not None:
            kw["accum_out"] = accum
        return self.op("scalar", lambda eh, o=out, i=in_, f=func, k=kw:
                       eh.activation(o, i, f, **k), waits=waits, inc=inc,
                       rd=[in_, bias, scale], wr=[out, accum])

    def mm(self, out, lhsT, rhs, start, stop, waits=(), inc=None):
        return self.op("tensor", lambda eh, o=out, l=lhsT, r=rhs, a=start, z=stop:
                       eh.matmul(o, l, r, start=a, stop=z), waits=waits, inc=inc)

    def wait(self, eng, *toks):
        self.op(eng, None, waits=toks)



def emit_p0(p, sb, dr, out):
    sd = [p.sem("p0d0", dma=True), p.sem("p0d1", dma=True)]
    sg = [p.sem("p0g0", dma=True), p.sem("p0g1", dma=True)]
    sv = p.sem("p0v")
    ss = p.sem("p0s")
    NT = TEXT // 128
    slot = [None, None]
    last_tmpa_reader = None
    last_stat_v = None
    st_all = sb["stats"]
    for tt in range(NT):
        bi = tt % 2
        own = tt >= NT // 2
        xt = sb["xt"][:, bi]
        st = st_all[:, bi * 24:(bi + 1) * 24]
        pv = slot[bi]
        tload = p.dma("sync", xt, dr["x_ext"][tt * 128:(tt + 1) * 128, :],
                      waits=([pv["t_xn"], pv["t_sum"]] if pv else []), inc=sd[bi])
        t_sq = p.act(sb["tmpa"][:], xt, AF.Square, accum=st[:, 0:1],
                     waits=[tload, last_tmpa_reader,
                            pv["t_stat"] if pv else None], inc=ss)
        t_sum = p.red(st[:, 1:2], xt,
                      waits=[tload] + ([pv["stores"], pv["t_xn"]] if pv else []),
                      inc=sv)
        p.ts(st[:, 2:3], st[:, 1:2], 1.0 / D)
        p.ts(st[:, 7:8], st[:, 0:1], 1.0 / D, waits=[t_sq])
        p.tt(st[:, 8:9], st[:, 2:3], st[:, 2:3], ALU.mult)
        p.tt(st[:, 9:10], st[:, 7:8], st[:, 8:9], ALU.subtract)
        t_u = p.ts(st[:, 10:11], st[:, 9:10], EPS_LN, op0=ALU.add, inc=sv)
        t_s0 = p.act(st[:, 12:13], st[:, 10:11], AF.Sqrt, waits=[t_u], inc=ss)
        p.recip(st[:, 3:4], st[:, 12:13], waits=[t_s0])
        for _ in range(2):
            p.tt(st[:, 13:14], st[:, 3:4], st[:, 3:4], ALU.mult)
            p.tt(st[:, 13:14], st[:, 13:14], st[:, 10:11], ALU.mult)
            p.ts(st[:, 13:14], st[:, 13:14], -0.5, 1.5, op0=ALU.mult, op1=ALU.add)
            p.tt(st[:, 3:4], st[:, 3:4], st[:, 13:14], ALU.mult)
        p.recip(st[:, 14:15], st[:, 10:11])
        p.tt(st[:, 15:16], st[:, 9:10], st[:, 14:15], ALU.mult)
        t_q = p.ts(st[:, 16:17], st[:, 15:16], EPS_RMS, op0=ALU.add, inc=sv)
        t_s1 = p.act(st[:, 17:18], st[:, 16:17], AF.Sqrt, waits=[t_q], inc=ss)
        p.recip(st[:, 4:5], st[:, 17:18], waits=[t_s1])
        for _ in range(2):
            p.tt(st[:, 13:14], st[:, 4:5], st[:, 4:5], ALU.mult)
            p.tt(st[:, 13:14], st[:, 13:14], st[:, 16:17], ALU.mult)
            p.ts(st[:, 13:14], st[:, 13:14], -0.5, 1.5, op0=ALU.mult, op1=ALU.add)
            p.tt(st[:, 4:5], st[:, 4:5], st[:, 13:14], ALU.mult)
        p.tt(st[:, 5:6], st[:, 10:11], st[:, 3:4], ALU.mult)
        p.tt(st[:, 6:7], st[:, 2:3], st[:, 3:4], ALU.mult)
        t_stat = p.ts(st[:, 6:7], st[:, 6:7], -1.0, inc=sv)
        t_xn = p.act(sb["tmpa"][:], xt, AF.Identity, bias=st[:, 6:7], scale=st[:, 3:4],
                     waits=[t_stat], inc=ss)
        t_xr = p.ts(sb["tmpb"][:], sb["tmpa"][:], st[:, 4:5], waits=[t_xn], inc=sv)
        t_xrh = p.copy(sb["hsta"][:, bi, 0], sb["tmpb"][:],
                       waits=[pv["stores"]] if pv else [], inc=sv)
        t_xrl = p.tt(sb["lsta"][:, bi, 0], sb["tmpb"][:], sb["hsta"][:, bi, 0],
                     ALU.subtract, inc=sv)
        trow = tt * 128
        p.dma("gpsimd", dr["xr_hi"][trow:trow + 128, :], sb["hsta"][:, bi, 0],
              waits=[t_xrh], inc=sg[bi])
        tok_st = p.dma("gpsimd", dr["xr_lo"][trow:trow + 128, :], sb["lsta"][:, bi, 0],
                       waits=[t_xrl], inc=sg[bi])
        last_tmpa = t_xr
        if own:
            orow = trow - TOK
            t_xnh = p.copy(sb["hsta"][:, bi, 1], sb["tmpa"][:], inc=sv)
            t_xnl = p.tt(sb["lsta"][:, bi, 1], sb["tmpa"][:], sb["hsta"][:, bi, 1],
                         ALU.subtract, inc=sv)
            p.dma("gpsimd", dr["xn_hi"][orow:orow + 128, :], sb["hsta"][:, bi, 1],
                  waits=[t_xnh], inc=sg[bi])
            p.dma("gpsimd", dr["xn_lo"][orow:orow + 128, :], sb["lsta"][:, bi, 1],
                  waits=[t_xnl], inc=sg[bi])
            p.dma("gpsimd", dr["mrow"][orow:orow + 128, :], st[:, 2:3],
                  waits=[t_stat], inc=sg[bi])
            tok_st = p.dma("gpsimd", dr["stdrow"][orow:orow + 128, :], st[:, 5:6],
                           waits=[t_stat], inc=sg[bi])
            last_tmpa = t_xnl
        slot[bi] = {"t_xn": t_xn, "t_sum": t_sum, "t_stat": t_stat, "stores": tok_st}
        last_tmpa_reader = last_tmpa
        last_stat_v = t_stat
        yield
    out["p0"] = [sg[0].tok(), sg[1].tok(), sv.tok(), ss.tok()]


def emit_prep(p, sb, dr, marks):
    sld = p.sem("prd", dma=True)
    sxb = p.sem("prx", dma=True)
    sst = {"qkv": p.sem("prs_qkv", dma=True)}
    for e in range(NE):
        sst[f"moe{e}"] = p.sem(f"prs_m{e}", dma=True)
    for g in range(8):
        sst[f"swg{g}"] = p.sem(f"prs_s{g}", dma=True)
    for j in range(4):
        sst[f"woj{j}"] = p.sem(f"prs_o{j}", dma=True)
    ssc = p.sem("prsc")
    svc = p.sem("prvc")
    st8 = {"hi": [None, None], "lo": [None, None], "ssem": None, "n": 0}

    def chunk_T(src_ap, dst_hi, dst_lo, w):
        cb = st8["n"] % 2
        st8["n"] += 1
        ws = sb["wslab"][:, cb, :D]
        tl = p.dma("sync", ws, src_ap, waits=[st8["hi"][cb], st8["lo"][cb]], inc=sld)
        th = p.act(sb["wshi"][:, :D], ws, AF.Identity,
                   waits=[tl, (sxb, sxb.val)], inc=ssc)
        tlo = p.tt(sb["wslo"][:, :D], ws, sb["wshi"][:, :D], ALU.subtract,
                   waits=[tl, th, (sxb, sxb.val)], inc=svc)
        ssem = st8["ssem"]
        sw0 = [(ssem, ssem.val)] if ssem is not None else []
        txh = p.dma_t("sync", sb["xbout"][:, 0], sb["wshi"][:, :D],
                      waits=[th] + sw0, inc=sxb)
        txl = p.dma_t("sync", sb["xbout"][:, 1], sb["wslo"][:, :D],
                      waits=[tlo] + sw0, inc=sxb)
        p.dma("gpsimd", dst_hi, sb["xbout"][:, 0], waits=[(sxb, sxb.val)], inc=sst[w])
        tsl = p.dma("gpsimd", dst_lo, sb["xbout"][:, 1], waits=[(sxb, sxb.val)],
                    inc=sst[w])
        st8["hi"][cb] = th
        st8["lo"][cb] = tlo
        st8["ssem"] = sst[w]
        marks[w] = tsl

    def chunk_N(src_ap, dst_hi, dst_lo, w, ncols):
        cb = st8["n"] % 2
        st8["n"] += 1
        ws = sb["wslab"][:, cb, :ncols]
        tl = p.dma("sync", ws, src_ap, waits=[st8["hi"][cb], st8["lo"][cb]], inc=sld)
        ssem = st8["ssem"]
        sw0 = [(ssem, ssem.val)] if ssem is not None else []
        th = p.act(sb["wshi"][:, :ncols], ws, AF.Identity,
                   waits=[tl] + sw0, inc=ssc)
        tlo = p.tt(sb["wslo"][:, :ncols], ws, sb["wshi"][:, :ncols], ALU.subtract,
                   waits=[tl, th] + sw0, inc=svc)
        p.dma("gpsimd", dst_hi, sb["wshi"][:, :ncols], waits=[th], inc=sst[w])
        tsl = p.dma("gpsimd", dst_lo, sb["wslo"][:, :ncols], waits=[tlo], inc=sst[w])
        st8["hi"][cb] = th
        st8["lo"][cb] = tlo
        st8["ssem"] = sst[w]
        marks[w] = tsl

    for fc in range(48):
        chunk_T(dr["w_qkv"][fc * 128:(fc + 1) * 128, :],
                dr["qkvT_hi"][fc], dr["qkvT_lo"][fc], "qkv")
        yield
    for e in range(NE):
        for rc in range(DC):
            rows = slice(rc * 128, (rc + 1) * 128)
            chunk_N(dr["w_gate"][e][rows, :], dr["gateN_hi"][e][rows, :],
                    dr["gateN_lo"][e][rows, :], f"moe{e}", FMOE)
            chunk_N(dr["w_up"][e][rows, :], dr["upN_hi"][e][rows, :],
                    dr["upN_lo"][e][rows, :], f"moe{e}", FMOE)
            yield
        for rc in range(6):
            rows = slice(rc * 128, (rc + 1) * 128)
            chunk_N(dr["w_down"][e][rows, :], dr["downN_hi"][e][rows, :],
                    dr["downN_lo"][e][rows, :], f"moe{e}", D)
            yield
        marks[f"moe{e}done"] = True
    for fp in range(64):
        for rc in (fp, 64 + fp):
            rows = slice(rc * 128, (rc + 1) * 128)
            chunk_N(dr["w_sw"][rows, :], dr["swN_hi"][rows, :],
                    dr["swN_lo"][rows, :], f"swg{fp // 8}", D)
        if fp % 8 == 7:
            marks[f"swgdone{fp // 8}"] = marks[f"swg{fp // 8}"]
        yield
    for j in range(4):
        cols = slice(j * 2048, (j + 1) * 2048)
        for rc in range(DC):
            rows = slice(rc * 128, (rc + 1) * 128)
            chunk_N(dr["w_out"][rows, cols], dr["woN_hi"][rows, cols],
                    dr["woN_lo"][rows, cols], f"woj{j}", 2048)
            yield
        marks[f"wojdone{j}"] = marks[f"woj{j}"]


def emit_p1(p, sb, dr, p0_toks, marks, out):
    sxr = p.sem("p1xr", dma=True)
    smk = p.sem("p1mk", dma=True)
    swt = [p.sem(f"p1w{i}", dma=True) for i in range(2)]
    srow = p.sem("p1row", dma=True)
    syst = [p.sem(f"p1y{i}", dma=True) for i in range(2)]
    stm = p.sem("p1t")
    sv = p.sem("p1v")
    ss = p.sem("p1s")

    p.memset(sb["carry"][:], 0.0)
    p.memset(sb["ones1"][:], 1.0)
    p.memset(sb["onesT"][:], 1.0, inc=sv)
    t_init = sv.tok()
    t_mask = p.dma("sync", sb["maskc"][:], dr["maskh"], inc=smk)

    state = {
        "wt_use": [None, None],
        "ps_free": [None, None],
        "sq_free": None,
        "pssq_free": None,
        "sc_scalar": None,
        "xrs_use": None,
    }
    qkv_ready = marks["qkv"]

    def mm_sweep(base_chunk, ssq):
        ev = []
        pend_ssq = None
        for i in range(DC):
            bi = i % 2
            tlh = p.dma("sync", sb["wtile"][:, bi, 0], dr["qkvT_hi"][base_chunk + i],
                        waits=[state["wt_use"][bi], qkv_ready]
                        if i < 2 else [state["wt_use"][bi]], inc=swt[bi])
            tll = p.dma("sync", sb["wtile"][:, bi, 1], dr["qkvT_lo"][base_chunk + i],
                        inc=swt[bi])
            pb = sb[f"pA{i % 2}"]
            w0 = [(swt[bi], swt[bi].val), (sxr, sxr.val), state["ps_free"][i % 2]]
            for dc in range(DC):
                p.mm(pb[:], sb["wtile"][:, bi, 0, dc], sb["xrs_hi"][:, dc],
                     start=(dc == 0), stop=False, waits=w0 if dc == 0 else [])
                p.mm(pb[:], sb["wtile"][:, bi, 0, dc], sb["xrs_lo"][:, dc],
                     start=False, stop=False)
                p.mm(pb[:], sb["wtile"][:, bi, 1, dc], sb["xrs_hi"][:, dc],
                     start=False, stop=(dc == DC - 1),
                     inc=stm if dc == DC - 1 else None)
            t_mm = stm.tok()
            state["wt_use"][bi] = t_mm
            state["xrs_use"] = t_mm
            if pend_ssq is not None:
                pi, t_sq = pend_ssq
                t_ssq = p.mm(sb["pA3"][0:1, :], sb["ones1"][:], sb["sq512"][:],
                             start=(pi == 0), stop=(pi == DC - 1),
                             waits=[t_sq, t_init, state["pssq_free"]]
                             if pi == 0 else [t_sq], inc=stm)
                state["sq_free"] = t_ssq
            t_ev = p.copy(sb["Sbuf"][:, i], pb[:], waits=[t_mm], inc=sv)
            ev.append(t_ev)
            state["ps_free"][i % 2] = t_ev
            if ssq:
                t_sq = p.act(sb["sq512"][:], sb["Sbuf"][:, i], AF.Square,
                             waits=[t_ev, state["sq_free"]], inc=ss)
                pend_ssq = (i, t_sq)
            yield
        if ssq and pend_ssq is not None:
            pi, t_sq = pend_ssq
            t_ssq = p.mm(sb["pA3"][0:1, :], sb["ones1"][:], sb["sq512"][:],
                         start=(pi == 0), stop=(pi == DC - 1),
                         waits=[t_sq], inc=stm)
            state["sq_free"] = t_ssq
        return ev

    def rsqrt_bc(bc, rowslot, tsl):
        t_ev = p.copy(sb["sc512d"][0:1, :], sb["pA3"][0:1, :],
                      waits=[state["sq_free"], state["sc_scalar"]], inc=sv)
        state["pssq_free"] = t_ev
        p.dma("gpsimd", dr["rows"][rowslot:rowslot + 1, tsl:tsl + 512],
              sb["sc512d"][0:1, :], waits=[t_ev], inc=srow)
        t_bc = p.dma("gpsimd", sb["sc512a"][:],
                     dr["rows"][rowslot, tsl:tsl + 512].partition_broadcast(128),
                     waits=[(srow, srow.val)], inc=srow)
        t_u = p.ts(sb["sc512a"][:], sb["sc512a"][:], 1.0 / D, EPS_RMS,
                   op0=ALU.mult, op1=ALU.add, waits=[t_bc], inc=sv)
        t_s0 = p.act(sb["sc512c"][:], sb["sc512a"][:], AF.Sqrt, waits=[t_u], inc=ss)
        state["sc_scalar"] = t_s0
        p.recip(bc, sb["sc512c"][:], waits=[t_s0])
        for _ in range(2):
            p.tt(sb["sc512b"][:], bc, bc, ALU.mult)
            p.tt(sb["sc512b"][:], sb["sc512b"][:], sb["sc512a"][:], ALU.mult)
            p.ts(sb["sc512b"][:], sb["sc512b"][:], -0.5, 1.5, op0=ALU.mult, op1=ALU.add)
            p.tt(bc, bc, sb["sc512b"][:], ALU.mult)
        return p.copy(sb["sc512b"][0:1, 0:1], bc[0:1, 0:1], inc=sv)

    for sl in range(4):
        own = sl >= 2
        tsl = sl * 512
        t_xh = p.dma_t("sync", sb["xrs_hi"][:],
                       dr["xr_hi"][tsl:tsl + 512, :],
                       waits=[state["xrs_use"]] + (list(p0_toks) if sl == 0 else []),
                       inc=sxr)
        t_xl = p.dma_t("sync", sb["xrs_lo"][:], dr["xr_lo"][tsl:tsl + 512, :],
                       inc=sxr)

        for _ in mm_sweep(16, ssq=True):
            yield
        t_rk = rsqrt_bc(sb["bc0"][:, :512], 0, tsl)
        yield
        w_tok = []
        for i in range(DC):
            tb = sb["sc512c"] if i % 2 == 0 else sb["sc512d"]
            t1 = p.tt(tb[:], sb["Sbuf"][:, i], sb["bc0"][:, :512], ALU.mult,
                      waits=[w_tok[i - 2]] if i >= 2 else [t_rk], inc=sv)
            t_w = p.act(sb["wbuf"][:, i], tb[:], AF.Exp, waits=[t1], inc=ss)
            state["sc_scalar"] = t_w
            w_tok.append(t_w)
            if i % 4 == 3:
                yield

        wv_tok = []
        for i in range(DC):
            bi = i % 2
            tlh = p.dma("sync", sb["wtile"][:, bi, 0], dr["qkvT_hi"][32 + i],
                        waits=[state["wt_use"][bi]], inc=swt[bi])
            p.dma("sync", sb["wtile"][:, bi, 1], dr["qkvT_lo"][32 + i], inc=swt[bi])
            pb = sb[f"pA{i % 2}"]
            w0 = [(swt[bi], swt[bi].val), state["ps_free"][i % 2]]
            for dc in range(DC):
                p.mm(pb[:], sb["wtile"][:, bi, 0, dc], sb["xrs_hi"][:, dc],
                     start=(dc == 0), stop=False, waits=w0 if dc == 0 else [])
                p.mm(pb[:], sb["wtile"][:, bi, 0, dc], sb["xrs_lo"][:, dc],
                     start=False, stop=False)
                p.mm(pb[:], sb["wtile"][:, bi, 1, dc], sb["xrs_hi"][:, dc],
                     start=False, stop=(dc == DC - 1),
                     inc=stm if dc == DC - 1 else None)
            t_mm = stm.tok()
            state["wt_use"][bi] = t_mm
            state["xrs_use"] = t_mm
            t_wv = p.tt(sb["wvbuf"][:, i], sb["wbuf"][:, i], pb[:], ALU.mult,
                        waits=[t_mm, w_tok[i]], inc=sv)
            wv_tok.append(t_wv)
            state["ps_free"][i % 2] = t_wv
            yield

        if not own:
            for i in range(DC):
                c0, c1 = 4 * (i % 2), 4 * (i % 2) + 1
                p.red(sb["ctmp"][:, c0:c0 + 1], sb["wbuf"][:, i],
                      waits=[w_tok[i], (smk, smk.val)])
                p.tt(sb["ctmp"][:, c1:c1 + 2 - 1], sb["ctmp"][:, c0:c0 + 1],
                     sb["maskc"][:], ALU.mult)
                p.tt(sb["carry"][:, 0, i:i + 1], sb["carry"][:, 0, i:i + 1],
                     sb["ctmp"][:, c1:c1 + 1], ALU.add)
                p.red(sb["ctmp"][:, c0 + 2:c0 + 3], sb["wvbuf"][:, i],
                      waits=[wv_tok[i]])
                p.tt(sb["ctmp"][:, c1 + 2:c1 + 3], sb["ctmp"][:, c0 + 2:c0 + 3],
                     sb["maskc"][:], ALU.mult)
                p.tt(sb["carry"][:, 1, i:i + 1], sb["carry"][:, 1, i:i + 1],
                     sb["ctmp"][:, c1 + 2:c1 + 3], ALU.add)
                if i % 4 == 3:
                    yield
            continue

        for _ in mm_sweep(0, ssq=True):
            yield
        t_rq = rsqrt_bc(sb["bc1"][:, :512], 1, tsl)
        yield

        off = (sl - 2) * 512
        for i in range(DC):
            rot = i % 2
            p.scan(sb["sc512a"][:], sb["onesT"][:, :512], sb["wbuf"][:, i],
                   sb["carry"][:, 0, i:i + 1], ALU.mult, ALU.add,
                   waits=[w_tok[i]] + ([state["sc_scalar"]] if i == 0 else []))
            p.copy(sb["carry"][:, 0, i:i + 1], sb["sc512a"][:, 511:512])
            p.scan(sb["sc512b"][:], sb["onesT"][:, :512], sb["wvbuf"][:, i],
                   sb["carry"][:, 1, i:i + 1], ALU.mult, ALU.add)
            p.copy(sb["carry"][:, 1, i:i + 1], sb["sc512b"][:, 511:512])
            p.ts(sb["sc512c"][:], sb["sc512a"][:], EPS_WC, op0=ALU.add)
            p.recip(sb["sc512c"][:], sb["sc512c"][:])
            t_qs = p.tt(sb["sc512d"][:], sb["Sbuf"][:, i], sb["bc1"][:, :512], ALU.mult,
                        waits=[t_rq], inc=sv)
            t_sig = p.act(sb["sq512"][:], sb["sc512d"][:], AF.Sigmoid,
                          waits=[t_qs, state["sq_free"]], inc=ss)
            state["sc_scalar"] = t_sig
            p.tt(sb["sc512d"][:], sb["sq512"][:], sb["sc512b"][:], ALU.mult,
                 waits=[t_sig])
            p.tt(sb["sc512d"][:], sb["sc512d"][:], sb["sc512c"][:], ALU.mult)
            t_h = p.copy(sb["yst"][:, rot, 0], sb["sc512d"][:],
                         waits=[(syst[rot], syst[rot].val)], inc=sv)
            t_l = p.tt(sb["yst"][:, rot, 1], sb["sc512d"][:], sb["yst"][:, rot, 0],
                       ALU.subtract, inc=sv)
            p.dma("gpsimd", dr["y_hi"][i][:, off:off + 512], sb["yst"][:, rot, 0],
                  waits=[t_h], inc=syst[rot])
            p.dma("gpsimd", dr["y_lo"][i][:, off:off + 512], sb["yst"][:, rot, 1],
                  waits=[t_l], inc=syst[rot])
            yield

    out["p1"] = [syst[0].tok(), syst[1].tok(), sv.tok(), ss.tok(), stm.tok(),
                 (srow, srow.val)]


def emit_p4(p, sb, dr, p1_toks, gprep, marks, out):
    sy = p.sem("p4y", dma=True)
    swl = [p.sem(f"p4w{i}", dma=True) for i in range(2)]
    shs = [p.sem(f"p4h{i}", dma=True) for i in range(2)]
    stm = p.sem("p4t")
    sv = p.sem("p4v")
    ss = p.sem("p4s")

    t_yh = p.dma("sync", sb["ybuf_hi"][:], dr["y_hi"].rearrange("c p t -> p c t"),
                 waits=list(p1_toks), inc=sy)
    t_yl = p.dma("sync", sb["ybuf_lo"][:], dr["y_lo"].rearrange("c p t -> p c t"),
                 inc=sy)
    wt_use = [None, None]
    ps_free = [None, None]
    silug_use = None
    hst_use = [None, None]
    t_first = None

    for fp in range(64):
        bi = fp % 2
        ur = slice(fp * 128, (fp + 1) * 128)
        gr = slice(DFF + fp * 128, DFF + (fp + 1) * 128)
        while f"swgdone{fp // 8}" not in marks:
            next(gprep)
        tw = p.dma_t("sync", sb["swtile"][:, bi, 0, 0], dr["swN_hi"][ur, :],
                     waits=[wt_use[bi], marks[f"swgdone{fp // 8}"]], inc=swl[bi])
        p.dma_t("sync", sb["swtile"][:, bi, 1, 0], dr["swN_lo"][ur, :], inc=swl[bi])
        p.dma_t("sync", sb["swtile"][:, bi, 0, 1], dr["swN_hi"][gr, :], inc=swl[bi])
        p.dma_t("sync", sb["swtile"][:, bi, 1, 1], dr["swN_lo"][gr, :], inc=swl[bi])
        for ug in range(2):
            pb = sb["pB0"] if ug == 0 else sb["pB1"]
            w0 = [(swl[bi], swl[bi].val), (sy, sy.val), ps_free[ug]]
            for dc in range(DC):
                for th in range(2):
                    hs = slice(th * 512, (th + 1) * 512)
                    p.mm(pb[:, hs], sb["swtile"][:, bi, 0, ug, dc],
                         sb["ybuf_hi"][:, dc, hs], start=(dc == 0), stop=False,
                         waits=w0 if dc == 0 and th == 0 else [])
                    p.mm(pb[:, hs], sb["swtile"][:, bi, 0, ug, dc],
                         sb["ybuf_lo"][:, dc, hs], start=False, stop=False)
                    p.mm(pb[:, hs], sb["swtile"][:, bi, 1, ug, dc],
                         sb["ybuf_hi"][:, dc, hs], start=False, stop=(dc == DC - 1),
                         inc=stm if (dc == DC - 1 and th == 1) else None)
            if ug == 0:
                t_u = stm.tok()
            else:
                t_g = stm.tok()
        wt_use[bi] = t_g
        t_sg = p.act(sb["silug"][:], sb["pB1"][:], AF.Silu,
                     waits=[t_g, silug_use], inc=ss)
        ps_free[1] = t_sg
        t_h = p.tt(sb["ht1"][:], sb["pB0"][:], sb["silug"][:], ALU.mult,
                   waits=[t_u, t_sg], inc=sv)
        ps_free[0] = t_h
        silug_use = t_h
        rot = fp % 2
        t_hh = p.copy(sb["hst"][:, rot, 0], sb["ht1"][:],
                      waits=[(shs[rot], shs[rot].val)], inc=sv)
        t_hl = p.tt(sb["hst"][:, rot, 1], sb["ht1"][:], sb["hst"][:, rot, 0],
                    ALU.subtract, inc=sv)
        p.dma("gpsimd", dr["h_hi"][fp], sb["hst"][:, rot, 0], waits=[t_hh],
              inc=shs[rot])
        p.dma("gpsimd", dr["h_lo"][fp], sb["hst"][:, rot, 1], waits=[t_hl],
              inc=shs[rot])
        yield

    out["p4"] = [shs[0].tok(), shs[1].tok(), sv.tok(), ss.tok(), stm.tok()]


def emit_p5(p, sb, dr, p4_toks, gprep, marks, out):
    sh = [p.sem(f"p5h{i}", dma=True) for i in range(2)]
    swl = [p.sem(f"p5w{i}", dma=True) for i in range(2)]
    sxn = [p.sem(f"p5x{i}", dma=True) for i in range(2)]
    sss = [p.sem(f"p5s{i}", dma=True) for i in range(2)]
    stm = p.sem("p5t")
    sv = p.sem("p5v")

    hs_use = [None, None]
    wt_use = [None, None]
    ps_free = [None, None, None, None]

    for hsuper in range(8):
        while f"wojdone{hsuper // 2}" not in marks:
            next(gprep)
        hb = hsuper % 2
        hrows = slice(8 * hsuper, 8 * hsuper + 8)
        t_h1 = p.dma("sync", sb["h5_hi"][:, hb],
                     dr["h_hi"].rearrange("c p t -> p c t")[:, hrows, :],
                     waits=[hs_use[hb]] + (list(p4_toks) if hsuper < 2 else []),
                     inc=sh[hb])
        p.dma("sync", sb["h5_lo"][:, hb],
              dr["h_lo"].rearrange("c p t -> p c t")[:, hrows, :], inc=sh[hb])
        for dout in range(DC):
            wb = dout % 2
            rows = slice(dout * 128, (dout + 1) * 128)
            cols = slice(hsuper * 1024, (hsuper + 1) * 1024)
            p.dma_t("sync", sb["wotile"][:, wb, 0], dr["woN_hi"][rows, cols],
                    waits=[wt_use[wb]] +
                    ([marks[f"wojdone{hsuper // 2}"]] if dout < 2 else []),
                    inc=swl[wb])
            p.dma_t("sync", sb["wotile"][:, wb, 1], dr["woN_lo"][rows, cols],
                    inc=swl[wb])
            for tb in range(2):
                bank = (dout % 2) * 2 + tb
                pb = sb[f"pA{bank}"]
                ts_ = slice(tb * 512, (tb + 1) * 512)
                w0 = [(swl[wb], swl[wb].val), (sh[hb], sh[hb].val), ps_free[bank]]
                for hc in range(8):
                    p.mm(pb[:], sb["wotile"][:, wb, 0, hc],
                         sb["h5_hi"][:, hb, hc, ts_], start=(hc == 0), stop=False,
                         waits=w0 if hc == 0 else [])
                    p.mm(pb[:], sb["wotile"][:, wb, 0, hc],
                         sb["h5_lo"][:, hb, hc, ts_], start=False, stop=False)
                    p.mm(pb[:], sb["wotile"][:, wb, 1, hc],
                         sb["h5_hi"][:, hb, hc, ts_], start=False, stop=(hc == 7),
                         inc=stm if hc == 7 else None)
                t_mm = stm.tok()
                wt_use[wb] = t_mm
                hs_use[hb] = t_mm
                if hsuper == 0:
                    t_acc = p.copy(sb["oacc"][:, dout, ts_], pb[:],
                                   waits=[t_mm], inc=sv)
                else:
                    t_acc = p.tt(sb["oacc"][:, dout, ts_], sb["oacc"][:, dout, ts_],
                                 pb[:], ALU.add, waits=[t_mm], inc=sv)
                ps_free[bank] = t_acc
            yield

    xn_use = [None, None]
    sst_use = [None, None]
    for dout in range(DC):
        bi = dout % 2
        p.dma_t("sync", sb["xn5h"][:, bi], dr["xn_hi"][:, dout * 128:(dout + 1) * 128],
                waits=[xn_use[bi]], inc=sxn[bi])
        p.dma_t("sync", sb["xn5l"][:, bi], dr["xn_lo"][:, dout * 128:(dout + 1) * 128],
                inc=sxn[bi])
        t_1 = p.tt(sb["sfin"][:, bi], sb["xn5h"][:, bi], sb["xn5l"][:, bi], ALU.add,
                   waits=[(sxn[bi], sxn[bi].val), (sss[bi], sss[bi].val)], inc=sv)
        xn_use[bi] = t_1
        t_s = p.tt(sb["sfin"][:, bi], sb["sfin"][:, bi], sb["oacc"][:, dout, :],
                   ALU.add, inc=sv)
        p.dma("gpsimd", dr["s_fm"][dout], sb["sfin"][:, bi], waits=[t_s], inc=sss[bi])
        yield

    out["p5"] = [sss[0].tok(), sss[1].tok(), sv.tok(), stm.tok(),
                 (sxn[0], sxn[0].val), (sxn[1], sxn[1].val)]


def emit_p3(p, sb, dr, prev_toks, gprep, marks, out):
    sxn = p.sem("p3xn", dma=True)
    swr = p.sem("p3wr", dma=True)
    sgu = [p.sem(f"p3g{i}", dma=True) for i in range(2)]
    sdt = p.sem("p3dt", dma=True)
    sce = p.sem("p3ce", dma=True)
    scf = p.sem("p3cf", dma=True)
    sma = [p.sem(f"p3m{i}", dma=True) for i in range(2)]
    sms = [p.sem(f"p3n{i}", dma=True) for i in range(2)]
    stm = p.sem("p3t")
    sv = p.sem("p3v")
    ss = p.sem("p3s")

    t_x1 = p.dma_t("sync", sb["xnf3h"][:], dr["xn_hi"],
                   waits=list(prev_toks), inc=sxn)
    t_x2 = p.dma_t("sync", sb["xnf3l"][:], dr["xn_lo"], inc=sxn)
    t_wr = p.dma("sync", sb["wrs"][:], dr["w_r"].rearrange("(c p) e -> p c e", p=128),
                 inc=swr)
    t_eye = p.dma("sync", sb["eyes"][:], dr["eye128"], inc=swr)
    t_io = p.dma("sync", sb["iotas"][:], dr["iota16"], inc=swr)
    t_wrh = p.act(sb["wrh"][:], sb["wrs"][:], AF.Identity,
                  waits=[(swr, swr.val)], inc=ss)
    t_wrl = p.tt(sb["wrl"][:], sb["wrs"][:], sb["wrh"][:], ALU.subtract,
                 waits=[t_wrh], inc=sv)
    yield

    ctile_use = None
    pA3_use = None
    for tt in range(8):
        pb = sb["pA2"]
        w0 = [(sxn, sxn.val), t_wrl, ctile_use]
        for dc in range(DC):
            l_hi = sb["xnf3h"][:, dc, tt * 128:(tt + 1) * 128]
            l_lo = sb["xnf3l"][:, dc, tt * 128:(tt + 1) * 128]
            p.mm(pb[:, :NE], l_hi, sb["wrh"][:, dc], start=(dc == 0), stop=False,
                 waits=w0 if dc == 0 else [])
            p.mm(pb[:, :NE], l_lo, sb["wrh"][:, dc], start=False, stop=False)
            p.mm(pb[:, :NE], l_hi, sb["wrl"][:, dc], start=False, stop=(dc == DC - 1),
                 inc=stm if dc == DC - 1 else None)
        t_mm = stm.tok()
        q = sb["rtq"]
        sc = q[:, 0:16]
        t_sc = p.copy(sc, pb[:, :NE], waits=[t_mm], inc=sv)
        p.red(q[:, 16:17], sc, alu=ALU.max)
        p.ts(q[:, 17:18], q[:, 16:17], -1.0)
        t_sub = p.ts(q[:, 20:36], sc, q[:, 17:18], op0=ALU.add, inc=sv)
        t_ex = p.act(q[:, 20:36], q[:, 20:36], AF.Exp, waits=[t_sub], inc=ss)
        t_sm = p.red(q[:, 18:19], q[:, 20:36], waits=[t_ex], inc=sv)
        p.recip(q[:, 19:20], q[:, 18:19])
        p.ts(q[:, 20:36], q[:, 20:36], q[:, 19:20], op0=ALU.mult)
        p.red(q[:, 36:37], q[:, 20:36], alu=ALU.max)
        p.ts(q[:, 40:56], q[:, 20:36], q[:, 36:37], op0=ALU.is_ge)
        p.ts(q[:, 40:56], q[:, 40:56], -1e6, 1e6, op0=ALU.mult, op1=ALU.add)
        p.tt(q[:, 40:56], q[:, 40:56], sb["iotas"][:], ALU.add)
        p.red(q[:, 37:38], q[:, 40:56], alu=ALU.min)
        p.ts(q[:, 40:56], sb["iotas"][:], q[:, 37:38], op0=ALU.is_equal)
        p.tt(q[:, 56:72], q[:, 40:56], q[:, 20:36], ALU.mult)
        p.tt(q[:, 56:72], q[:, 20:36], q[:, 56:72], ALU.subtract)
        p.red(q[:, 38:39], q[:, 56:72], alu=ALU.max)
        p.tt(q[:, 39:40], q[:, 36:37], q[:, 38:39], ALU.add)
        p.recip(q[:, 39:40], q[:, 39:40])
        p.ts(q[:, 56:72], q[:, 56:72], q[:, 38:39], op0=ALU.is_ge)
        p.ts(q[:, 56:72], q[:, 56:72], -1e6, 1e6, op0=ALU.mult, op1=ALU.add)
        p.tt(q[:, 56:72], q[:, 56:72], sb["iotas"][:], ALU.add)
        p.red(q[:, 37:38], q[:, 56:72], alu=ALU.min)
        p.ts(q[:, 56:72], sb["iotas"][:], q[:, 37:38], op0=ALU.is_equal)
        p.tt(q[:, 72:73], q[:, 36:37], q[:, 39:40], ALU.mult)
        p.tt(q[:, 73:74], q[:, 38:39], q[:, 39:40], ALU.mult)
        p.ts(q[:, 40:56], q[:, 40:56], q[:, 72:73], op0=ALU.mult)
        p.ts(q[:, 56:72], q[:, 56:72], q[:, 73:74], op0=ALU.mult)
        t_c = p.tt(q[:, 40:56], q[:, 40:56], q[:, 56:72], ALU.add, inc=sv)
        t_tr = p.op("tensor",
                    lambda eh, o=sb["pA3"][0:NE, 0:128], i=q[:, 40:56],
                    e=sb["eyes"][:]: eh.transpose(o, i, e),
                    waits=[t_c, (swr, swr.val), pA3_use], inc=stm)
        t_cf = p.copy(sb["cfm"][0:NE, tt * 128:(tt + 1) * 128],
                      sb["pA3"][0:NE, 0:128], waits=[t_tr], inc=sv)
        ctile_use = t_cf
        pA3_use = t_cf
        last_sc = t_sc
        yield
    p.dma("gpsimd", dr["c_fm"], sb["cfm"][0:NE, :], waits=[ctile_use], inc=scf)

    gu_par_use = [None, None]
    gus_slot = [None, None]
    dt_mm_use = None
    dts_split_use = None
    ps_freeA = [last_sc, last_sc, last_sc, pA3_use]
    ps_freeB = [None, None]
    ce_use = None
    htil_use = None
    macc_use = [None, None]
    silug_use = [None, None]
    chain_rot = 0
    fch_ctr = 0
    for e in range(NE):
        while f"moe{e}done" not in marks:
            next(gprep)
        t_t2 = None
        t_ce = p.dma("gpsimd", sb["cebc"][:],
                     dr["c_fm"][e, :].partition_broadcast(128),
                     waits=[ce_use, (scf, scf.val)], inc=sce)
        for fch in range(3):
            par = fch_ctr % 2
            fch_ctr += 1
            cs = slice(fch * 256, (fch + 1) * 256)
            me = marks[f"moe{e}"]
            p.dma("sync", sb["guh"][:, par, 0],
                  dr["gateN_hi"][e][:, cs].rearrange("(c p) f -> p c f", p=128),
                  waits=[gu_par_use[par], me], inc=sgu[par])
            p.dma("sync", sb["gul"][:, par, 0],
                  dr["gateN_lo"][e][:, cs].rearrange("(c p) f -> p c f", p=128),
                  inc=sgu[par])
            p.dma("sync", sb["guh"][:, par, 1],
                  dr["upN_hi"][e][:, cs].rearrange("(c p) f -> p c f", p=128),
                  inc=sgu[par])
            p.dma("sync", sb["gul"][:, par, 1],
                  dr["upN_lo"][e][:, cs].rearrange("(c p) f -> p c f", p=128),
                  inc=sgu[par])
            t_split = (sgu[par], sgu[par].val)
            for fc3 in range(2):
              fc = fch * 2 + fc3
              fs = slice(fc3 * 128, (fc3 + 1) * 128)
              for th in range(2):
                hs = slice(th * 512, (th + 1) * 512)
                pg = sb[f"pA{2 * th}"]
                pu = sb[f"pA{2 * th + 1}"]
                w0 = [t_split, (sxn, sxn.val), ps_freeA[2 * th],
                      ps_freeA[2 * th + 1]]
                for dc in range(DC):
                    p.mm(pg[:], sb["guh"][:, par, 0, dc, fs], sb["xnf3h"][:, dc, hs],
                         start=(dc == 0), stop=False, waits=w0 if dc == 0 else [])
                    p.mm(pg[:], sb["guh"][:, par, 0, dc, fs], sb["xnf3l"][:, dc, hs],
                         start=False, stop=False)
                    p.mm(pg[:], sb["gul"][:, par, 0, dc, fs], sb["xnf3h"][:, dc, hs],
                         start=False, stop=(dc == DC - 1))
                    p.mm(pu[:], sb["guh"][:, par, 1, dc, fs], sb["xnf3h"][:, dc, hs],
                         start=(dc == 0), stop=False)
                    p.mm(pu[:], sb["guh"][:, par, 1, dc, fs], sb["xnf3l"][:, dc, hs],
                         start=False, stop=False)
                    p.mm(pu[:], sb["gul"][:, par, 1, dc, fs], sb["xnf3h"][:, dc, hs],
                         start=False, stop=(dc == DC - 1),
                         inc=stm if dc == DC - 1 else None)
                t_mm = stm.tok()
                gu_par_use[par] = t_mm
                cr = chain_rot % 2
                chain_rot += 1
                csl = slice(cr * 512, (cr + 1) * 512)
                t_sg = p.act(sb["silug"][:, csl], pg[:], AF.Silu,
                             waits=[t_mm, silug_use[cr]], inc=ss)
                t_t1 = p.tt(sb["mt1"][:, csl], pu[:], sb["silug"][:, csl],
                            ALU.mult, waits=[t_sg, (sce, sce.val)], inc=sv)
                ps_freeA[2 * th] = t_t1
                ps_freeA[2 * th + 1] = t_t1
                silug_use[cr] = t_t1
                t_t2 = p.tt(sb["mt1"][:, csl], sb["mt1"][:, csl],
                            sb["cebc"][:, hs], ALU.mult, inc=sv)
                t_hh = p.copy(sb["htil_hi"][:, fc, hs], sb["mt1"][:, csl],
                              waits=[htil_use] if (fch, fc3, th) == (0, 0, 0) else [],
                              inc=sv)
                p.tt(sb["htil_lo"][:, fc, hs], sb["mt1"][:, csl],
                     sb["htil_hi"][:, fc, hs], ALU.subtract, inc=sv)
            yield
        ce_use = t_t2
        t_hready = sv.tok()

        for dgc in range(DC):
            p.dma("sync", sb["dth"][:],
                  dr["downN_hi"][e].rearrange("(c p) d -> p c d", p=128)
                  [:, :, dgc * 128:(dgc + 1) * 128],
                  waits=[dt_mm_use], inc=sdt)
            p.dma("sync", sb["dtl"][:],
                  dr["downN_lo"][e].rearrange("(c p) d -> p c d", p=128)
                  [:, :, dgc * 128:(dgc + 1) * 128], inc=sdt)
            tl_d = (sdt, sdt.val)
            pb = sb[f"pB{dgc % 2}"]
            mrot = dgc % 2
            if e > 0:
                p.dma("gpsimd", sb["macc"][:, mrot], dr["moe_fm"][dgc],
                      waits=[macc_use[mrot], (sms[mrot], sms[mrot].val)],
                      inc=sma[mrot])
            w0 = [tl_d, t_hready, ps_freeB[dgc % 2]]
            for fc in range(6):
                for th in range(2):
                    hs = slice(th * 512, (th + 1) * 512)
                    p.mm(pb[:, hs], sb["dth"][:, fc], sb["htil_hi"][:, fc, hs],
                         start=(fc == 0), stop=False, waits=w0 if fc == 0 else [])
                    p.mm(pb[:, hs], sb["dth"][:, fc], sb["htil_lo"][:, fc, hs],
                         start=False, stop=False)
                    p.mm(pb[:, hs], sb["dtl"][:, fc], sb["htil_hi"][:, fc, hs],
                         start=False, stop=(fc == 5),
                         inc=stm if (fc, th) == (5, 1) else None)
            t_mm = stm.tok()
            dt_mm_use = t_mm
            htil_use = t_mm
            if e == 0:
                t_acc = p.copy(sb["macc"][:, mrot], pb[:],
                               waits=[t_mm, (sms[mrot], sms[mrot].val),
                                      macc_use[mrot]], inc=sv)
            else:
                t_acc = p.tt(sb["macc"][:, mrot], sb["macc"][:, mrot], pb[:],
                             ALU.add, waits=[t_mm, (sma[mrot], sma[mrot].val)],
                             inc=sv)
            ps_freeB[dgc % 2] = t_acc
            macc_use[mrot] = t_acc
            p.dma("gpsimd", dr["moe_fm"][dgc], sb["macc"][:, mrot],
                  waits=[t_acc], inc=sms[mrot])
            yield

    out["p3"] = [sms[0].tok(), sms[1].tok(), sv.tok(), ss.tok(), stm.tok(),
                 (sce, sce.val), (scf, scf.val), (sxn, sxn.val)]


def emit_p6(p, sb, dr, prev_toks, out):
    sl6 = [p.sem(f"p6l{i}", dma=True) for i in range(2)]
    sbc = p.sem("p6bc", dma=True)
    sos = [p.sem(f"p6o{i}", dma=True) for i in range(2)]
    sv = p.sem("p6v")
    ss = p.sem("p6s")

    p.dma("gpsimd", sb["mbc"][:], dr["mrow"][:, 0].partition_broadcast(128),
          waits=list(prev_toks), inc=sbc)
    p.dma("gpsimd", sb["stdbc"][:], dr["stdrow"][:, 0].partition_broadcast(128),
          inc=sbc)
    buf_use = [None, None]
    g6_use = None
    for dc in range(DC):
        bi = dc % 2
        p.dma("sync", sb["s6"][:, bi], dr["s_fm"][dc],
              waits=[buf_use[bi]] + (list(prev_toks) if dc < 2 else []),
              inc=sl6[bi])
        p.dma("sync", sb["m6"][:, bi], dr["moe_fm"][dc], inc=sl6[bi])
        p.dma_t("sync", sb["xn6h"][:, bi], dr["xn_hi"][:, dc * 128:(dc + 1) * 128],
                inc=sl6[bi])
        p.dma_t("sync", sb["xn6l"][:, bi], dr["xn_lo"][:, dc * 128:(dc + 1) * 128],
                inc=sl6[bi])
        t_g = p.act(sb["g6"][:], sb["m6"][:, bi], AF.Silu,
                    waits=[(sl6[bi], sl6[bi].val), g6_use], inc=ss)
        t_1 = p.tt(sb["t6"][:], sb["xn6h"][:, bi], sb["xn6l"][:, bi], ALU.add,
                   waits=[(sl6[bi], sl6[bi].val), (sbc, sbc.val)], inc=sv)
        p.tt(sb["t6"][:], sb["t6"][:], sb["stdbc"][:], ALU.mult)
        p.tt(sb["t6"][:], sb["t6"][:], sb["mbc"][:], ALU.add)
        t_4 = p.tt(sb["u6"][:], sb["s6"][:, bi], sb["g6"][:], ALU.mult,
                   waits=[t_g], inc=sv)
        g6_use = t_4
        t_o = p.tt(sb["o6"][:, bi], sb["t6"][:], sb["u6"][:], ALU.add,
                   waits=[(sos[bi], sos[bi].val)], inc=sv)
        buf_use[bi] = t_o
        p.dma("gpsimd", dr["out_fm"][dc], sb["o6"][:, bi], waits=[t_o], inc=sos[bi])
        yield

    out["p6"] = [sos[0].tok(), sos[1].tok()]



def build_nc(dev=False):
    nc = bass.Bass("TRN2", target_bir_lowering=False, detect_race_conditions=False)
    p = Prog()

    dr = {}
    dr["x_ext"] = nc.dram_tensor("x_ext", [TEXT, D], F32, kind="ExternalInput").ap()
    dr["w_qkv"] = nc.dram_tensor("w_qkv", [3 * D, D], F32, kind="ExternalInput").ap()
    dr["w_sw"] = nc.dram_tensor("w_sw", [2 * DFF, D], F32, kind="ExternalInput").ap()
    dr["w_out"] = nc.dram_tensor("w_out", [D, DFF], F32, kind="ExternalInput").ap()
    dr["w_r"] = nc.dram_tensor("w_r", [D, NE], F32, kind="ExternalInput").ap()
    dr["w_gate"] = nc.dram_tensor("w_gate", [NE, D, FMOE], F32, kind="ExternalInput").ap()
    dr["w_up"] = nc.dram_tensor("w_up", [NE, D, FMOE], F32, kind="ExternalInput").ap()
    dr["w_down"] = nc.dram_tensor("w_down", [NE, FMOE, D], F32, kind="ExternalInput").ap()
    dr["maskh"] = nc.dram_tensor("maskh", [128, 1], F32, kind="ExternalInput").ap()
    dr["iota16"] = nc.dram_tensor("iota16", [128, NE], F32, kind="ExternalInput").ap()
    dr["eye128"] = nc.dram_tensor("eye128", [128, 128], F32, kind="ExternalInput").ap()

    dr["out_fm"] = nc.dram_tensor("out_fm", [DC, 128, TOK], F32, kind="ExternalOutput").ap()

    def scr(name, shape, dt):
        if name in DEV_OUT:
            dr[name] = nc.dram_tensor(name, shape, dt, kind="ExternalOutput").ap()
        else:
            dr[name] = nc.dram_tensor(name, shape, dt).ap()
        return dr[name]

    scr("xr_hi", [TEXT, D], BF16)
    scr("xr_lo", [TEXT, D], BF16)
    scr("xn_hi", [TOK, D], BF16)
    scr("xn_lo", [TOK, D], BF16)
    scr("mrow", [TOK, 1], F32)
    scr("stdrow", [TOK, 1], F32)
    scr("rows", [8, TEXT], F32)
    scr("qkvT_hi", [48, 128, DC, 128], BF16)
    scr("qkvT_lo", [48, 128, DC, 128], BF16)
    scr("swN_hi", [2 * DFF, D], BF16)
    scr("swN_lo", [2 * DFF, D], BF16)
    scr("woN_hi", [D, DFF], BF16)
    scr("woN_lo", [D, DFF], BF16)
    scr("gateN_hi", [NE, D, FMOE], BF16)
    scr("gateN_lo", [NE, D, FMOE], BF16)
    scr("upN_hi", [NE, D, FMOE], BF16)
    scr("upN_lo", [NE, D, FMOE], BF16)
    scr("downN_hi", [NE, FMOE, D], BF16)
    scr("downN_lo", [NE, FMOE, D], BF16)
    scr("y_hi", [DC, 128, TOK], BF16)
    scr("y_lo", [DC, 128, TOK], BF16)
    scr("h_hi", [64, 128, TOK], BF16)
    scr("h_lo", [64, 128, TOK], BF16)
    scr("s_fm", [DC, 128, TOK], F32)
    scr("moe_fm", [DC, 128, TOK], F32)
    scr("c_fm", [NE, TOK], F32)
    scr("dbg", [16, 128, 24], F32)

    with contextlib.ExitStack() as es:
        sb = {}
        ARENA = 204 * 1024
        arena_base = (nc.sbuf_base + 31) // 32 * 32
        nc.alloc_sbuf_tensor("arena", [128, ARENA + 64], mybir.dt.uint8)

        def place(name, shape, dt, off):
            assert off % 32 == 0, name
            t = nc.alloc_sbuf_tensor_at(name, shape, dt, offset=arena_base + off)
            nbytes = int(np.prod(shape[1:])) * (2 if dt == BF16 else 4)
            assert off + nbytes <= ARENA, (name, off, nbytes)
            sb[name] = t
            return t

        G = 0
        PREP = 512
        R = 25088

        place("ones1", [128, 1], F32, G + 0)
        place("rowst", [128, 16], F32, G + 32)
        place("carry", [128, 2, DC], F32, G + 96)
        place("ctmp", [128, 8], F32, G + 224)
        place("maskc", [128, 1], F32, G + 288)

        place("wslab", [128, 2, D], F32, PREP + 0)
        place("wshi", [128, D], BF16, PREP + 16384)
        place("wslo", [128, D], BF16, PREP + 20480)
        place("xbout", [128, 2, DC, 128], BF16, R + 165888)

        place("xt", [128, 2, D], F32, R + 0)
        place("tmpa", [128, D], F32, R + 16384)
        place("tmpb", [128, D], F32, R + 24576)
        place("hsta", [128, 2, 2, D], BF16, R + 32768)
        place("lsta", [128, 2, 2, D], BF16, R + 49152)
        place("stats", [128, 48], F32, R + 65536)
        place("xrs_hi", [128, DC, 512], BF16, R + 0)
        place("xrs_lo", [128, DC, 512], BF16, R + 16384)
        place("wtile", [128, 2, 2, DC, 128], BF16, R + 32768)
        place("Sbuf", [128, DC, 512], F32, R + 49152)
        place("wbuf", [128, DC, 512], F32, R + 81920)
        place("wvbuf", [128, DC, 512], F32, R + 114688)
        place("sq512", [128, 512], F32, R + 147456)
        place("bc0", [128, 512], F32, R + 149504)
        place("bc1", [128, 512], F32, R + 153600)
        place("onesT", [128, TOK], F32, R + 157696)
        place("sc512a", [128, 512], F32, R + 161792)
        place("sc512b", [128, 512], F32, R + 163840)
        place("sc512c", [128, 512], F32, R + 165888)
        place("sc512d", [128, 512], F32, R + 167936)
        place("yst", [128, 2, 2, 512], BF16, R + 169984)
        place("ybuf_hi", [128, DC, 1024], BF16, R + 0)
        place("ybuf_lo", [128, DC, 1024], BF16, R + 32768)
        place("swtile", [128, 2, 2, 2, DC, 128], BF16, R + 65536)
        place("silug", [128, 1024], F32, R + 98304)
        place("ht1", [128, 1024], F32, R + 102400)
        place("hst", [128, 2, 2, 1024], BF16, R + 106496)
        place("oacc", [128, DC, 1024], F32, R + 0)
        place("h5_hi", [128, 2, 8, 1024], BF16, R + 65536)
        place("h5_lo", [128, 2, 8, 1024], BF16, R + 98304)
        place("wotile", [128, 2, 2, 8, 128], BF16, R + 131072)
        place("xn5h", [128, 2, 1024], BF16, R + 139264)
        place("xn5l", [128, 2, 1024], BF16, R + 143360)
        place("sfin", [128, 2, 1024], F32, R + 147456)
        place("xnf3h", [128, DC, 1024], BF16, R + 0)
        place("xnf3l", [128, DC, 1024], BF16, R + 32768)
        place("htil_hi", [128, 6, 1024], BF16, R + 65536)
        place("htil_lo", [128, 6, 1024], BF16, R + 77824)
        place("guh", [128, 2, 2, DC, 256], BF16, R + 90112)
        place("gul", [128, 2, 2, DC, 256], BF16, R + 122880)
        place("silug", [128, 1024], F32, R + 155648)
        place("mt1", [128, 1024], F32, R + 159744)
        place("cebc", [128, 1024], F32, R + 163840)
        place("macc", [128, 2, 1024], F32, R + 167936)
        place("dth", [128, 6, 128], BF16, R + 176128)
        place("dtl", [128, 6, 128], BF16, R + 177664)
        place("wrs", [128, DC, NE], F32, R + 179200)
        place("wrh", [128, DC, NE], BF16, R + 180224)
        place("wrl", [128, DC, NE], BF16, R + 180736)
        place("eyes", [128, 128], F32, R + 181248)
        place("iotas", [128, NE], F32, R + 181760)
        place("rtq", [128, 80], F32, R + 181888)
        place("cfm", [128, 1024], F32, R + 167936)
        place("s6", [128, 2, 1024], F32, R + 0)
        place("m6", [128, 2, 1024], F32, R + 8192)
        place("xn6h", [128, 2, 1024], BF16, R + 16384)
        place("xn6l", [128, 2, 1024], BF16, R + 20480)
        place("mbc", [128, 1024], F32, R + 24576)
        place("stdbc", [128, 1024], F32, R + 28672)
        place("g6", [128, 1024], F32, R + 32768)
        place("t6", [128, 1024], F32, R + 36864)
        place("u6", [128, 1024], F32, R + 40960)
        place("o6", [128, 2, 1024], F32, R + 45056)

        def psum(name, shape):
            sb[name] = es.enter_context(nc.psum_tensor(name, shape, F32))
        for i in range(4):
            psum(f"pA{i}", [128, 512])
        for i in range(2):
            psum(f"pB{i}", [128, 1024])

        marks, out = {}, {}
        gprep = emit_prep(p, sb, dr, marks)
        for _ in emit_p0(p, sb, dr, out):
            for _ in range(3):
                next(gprep, None)
        for _ in emit_p1(p, sb, dr, out["p0"], marks, out):
            pass
        for _ in emit_p3(p, sb, dr, out["p1"], gprep, marks, out):
            next(gprep, None)
        for _ in emit_p4(p, sb, dr, out["p3"], gprep, marks, out):
            next(gprep, None)
        for _ in emit_p5(p, sb, dr, out["p4"], gprep, marks, out):
            next(gprep, None)
        for _ in gprep:
            pass
        for _ in emit_p6(p, sb, dr, list(out["p5"]) + list(out["p3"]), out):
            pass
        p.wait("gpsimd", *out["p6"])

        for s in p.sems:
            s.h = es.enter_context(nc.semaphore(s.name))
        block = es.enter_context(nc.Block())

        for eng in ENGINES:
            def emit(eh, eng=eng):
                for f in p.q[eng]:
                    f(eh)
            getattr(block, eng)(emit)

    return nc



def kernel(L0_attn_x, L0_attn_w_qkv, L0_attn_w_swiglu, L0_attn_w_out,
           L0_ffn_router_w, L0_ffn_gate_w, L0_ffn_up_w, L0_ffn_down_w):
    x = np.ascontiguousarray(np.asarray(L0_attn_x, dtype=np.float32))
    wq = np.ascontiguousarray(np.asarray(L0_attn_w_qkv, dtype=np.float32))
    wsw = np.ascontiguousarray(np.asarray(L0_attn_w_swiglu, dtype=np.float32))
    wo = np.ascontiguousarray(np.asarray(L0_attn_w_out, dtype=np.float32))
    wr = np.ascontiguousarray(np.asarray(L0_ffn_router_w, dtype=np.float32))
    wg = np.ascontiguousarray(np.asarray(L0_ffn_gate_w, dtype=np.float32))
    wu = np.ascontiguousarray(np.asarray(L0_ffn_up_w, dtype=np.float32))
    wd = np.ascontiguousarray(np.asarray(L0_ffn_down_w, dtype=np.float32))

    iota16 = np.tile(np.arange(NE, dtype=np.float32), (128, 1))
    eye128 = np.eye(128, dtype=np.float32)

    nc = build_nc()
    in_maps = []
    for c in range(8):
        b, h = c // 2, c % 2
        own = x[b, h * TOK:(h + 1) * TOK]
        prev = x[b, 0:TOK] if h == 1 else own
        x_ext = np.concatenate([prev, own], axis=0)
        in_maps.append(dict(
            x_ext=x_ext, w_qkv=wq, w_sw=wsw, w_out=wo, w_r=wr,
            w_gate=wg, w_up=wu, w_down=wd,
            maskh=np.full((128, 1), float(h), np.float32),
            iota16=iota16, eye128=eye128,
        ))
    res = run_bass_kernel_spmd(nc, in_maps, core_ids=list(range(8)))
    out = np.empty((B, S, D), np.float32)
    for c in range(8):
        b, h = c // 2, c % 2
        o = np.asarray(res.results[c]["out_fm"])
        out[b, h * TOK:(h + 1) * TOK, :] = o.reshape(D, TOK).T
    return out


# version 09 (final)
# speedup vs baseline: 1.3347x; 1.0236x over previous
import contextlib
import numpy as np
import ml_dtypes
import concourse.bass as bass
import concourse.mybir as mybir
from concourse.bass_utils import run_bass_kernel_spmd

F32 = mybir.dt.float32
BF16 = mybir.dt.bfloat16
AF = mybir.ActivationFunctionType
ALU = mybir.AluOpType
AX = mybir.AxisListType

B, S, D = 4, 2048, 2048
TOK = 1024
TEXT = 2048
DC = D // 128
DFF = 8192
NE, FMOE, KTOP = 16, 768, 2
EPS_RMS = 1.1920929e-07
EPS_LN = 1e-5
EPS_WC = 1e-6

ENGINES = ("sync", "scalar", "vector", "tensor", "gpsimd")
DEV_OUT = ()


class Sem:
    def __init__(self, name, dma=False):
        self.name = name
        self.dma = dma
        self.val = 0
        self.h = None

    def tok(self):
        return (self, self.val)


class Prog:

    def __init__(self):
        self.q = {e: [] for e in ENGINES}
        self.sems = []
        self.pending = {e: set() for e in ENGINES}
        self.last_w = {e: {} for e in ENGINES}
        self.hz = None

    def sem(self, name, dma=False):
        s = Sem(name, dma)
        self.sems.append(s)
        return s

    @staticmethod
    def _name(x):
        t = getattr(x, "tensor", None)
        if t is not None:
            return getattr(t, "name", None)
        if hasattr(x, "shape") and hasattr(x, "name"):
            return x.name
        return None

    def _names(self, xs):
        out = set()
        for x in xs:
            n = self._name(x)
            if n is not None:
                out.add(n)
        return out

    def op(self, eng, fn, waits=(), inc=None, inc_n=1, rd=(), wr=()):
        waits = list(w for w in waits if w is not None)
        rdn = self._names(rd)
        wrn = self._names(wr)
        if eng == "vector" and rdn & self.pending[eng]:
            self.q[eng].append(lambda eh: eh.drain())
            self.pending[eng] = set()
        elif eng == "scalar":
            hazard = rdn & self.pending[eng]
            for n in hazard:
                t = self.last_w[eng].get(n)
                if t is not None:
                    waits.append(t)
                self.pending[eng].discard(n)
            if wrn and inc is None:
                if self.hz is None:
                    self.hz = self.sem("hzs")
                inc = self.hz
                inc_n = 1
        for s, v in waits:
            assert not s.dma or v == s.val, (
                f"partial wait on dma sem {s.name}: {v} < {s.val}")
        if inc is not None:
            inc.val += inc_n

        def run(eh, fn=fn, waits=waits, inc=inc, inc_n=inc_n):
            for s, v in waits:
                if v > 0:
                    eh.wait_ge(s.h, v)
            instr = fn(eh) if fn is not None else None
            if inc is not None:
                assert instr is not None, "inc requires an instruction"
                instr.then_inc(inc.h, inc_n)
            return instr

        self.q[eng].append(run)
        self.pending[eng] |= wrn
        if eng == "scalar" and inc is not None:
            for n in wrn:
                self.last_w[eng][n] = inc.tok()
        return inc.tok() if inc is not None else None

    def dma(self, eng, out, in_, waits=(), inc=None):
        return self.op(eng, lambda eh, o=out, i=in_: eh.dma_start(out=o, in_=i),
                       waits=waits, inc=inc, inc_n=16)

    def dma_t(self, eng, out, in_, waits=(), inc=None):
        return self.op(eng, lambda eh, o=out, i=in_: eh.dma_start_transpose(out=o, in_=i),
                       waits=waits, inc=inc, inc_n=16)

    def tt(self, out, a, b, alu, waits=(), inc=None, eng="vector"):
        return self.op(eng, lambda eh, o=out, x=a, y=b, u=alu:
                       eh.tensor_tensor(o, x, y, op=u), waits=waits, inc=inc,
                       rd=[a, b], wr=[out])

    def ts(self, out, in0, s1, s2=None, op0=ALU.mult, op1=None, waits=(), inc=None,
           eng="vector", accum=None):
        kw = {}
        if op1 is not None:
            kw["op1"] = op1
        if accum is not None:
            kw["accum_out"] = accum
        return self.op(eng, lambda eh, o=out, i=in0, a=s1, b=s2, u=op0, k=kw:
                       eh.tensor_scalar(out=o, in0=i, scalar1=a, scalar2=b, op0=u, **k),
                       waits=waits, inc=inc, rd=[in0, s1, s2], wr=[out, accum])

    def copy(self, out, in_, waits=(), inc=None, eng="vector"):
        return self.op(eng, lambda eh, o=out, i=in_: eh.tensor_copy(o, i),
                       waits=waits, inc=inc, rd=[in_], wr=[out])

    def recip(self, out, in_, waits=(), inc=None):
        return self.op("vector", lambda eh, o=out, i=in_: eh.reciprocal(o, i),
                       waits=waits, inc=inc, rd=[in_], wr=[out])

    def red(self, out, in_, alu=ALU.add, waits=(), inc=None, negate=None):
        if alu == ALU.add:
            return self.op("vector", lambda eh, o=out, i=in_:
                           eh.reduce_sum(o, i, axis=AX.X), waits=waits, inc=inc,
                           rd=[in_], wr=[out])
        return self.op("vector", lambda eh, o=out, i=in_, u=alu, n=negate:
                       eh.tensor_reduce(o, i, axis=AX.X, op=u, negate=n),
                       waits=waits, inc=inc, rd=[in_], wr=[out])

    def scan(self, out, data0, data1, initial, op0, op1, waits=(), inc=None):
        return self.op("vector", lambda eh, o=out, d0=data0, d1=data1, i=initial,
                       u0=op0, u1=op1:
                       eh.tensor_tensor_scan(o, d0, d1, i, op0=u0, op1=u1),
                       waits=waits, inc=inc, rd=[data0, data1, initial], wr=[out])

    def memset(self, ap, val, waits=(), inc=None, eng="vector"):
        return self.op(eng, lambda eh, a=ap, v=val: eh.memset(a, v),
                       waits=waits, inc=inc, wr=[ap])

    def act(self, out, in_, func, bias=None, scale=None, accum=None, waits=(), inc=None):
        kw = {}
        if bias is not None:
            kw["bias"] = bias
        if scale is not None:
            kw["scale"] = scale
        if accum is not None:
            kw["accum_out"] = accum
        return self.op("scalar", lambda eh, o=out, i=in_, f=func, k=kw:
                       eh.activation(o, i, f, **k), waits=waits, inc=inc,
                       rd=[in_, bias, scale], wr=[out, accum])

    def mm(self, out, lhsT, rhs, start, stop, waits=(), inc=None):
        return self.op("tensor", lambda eh, o=out, l=lhsT, r=rhs, a=start, z=stop:
                       eh.matmul(o, l, r, start=a, stop=z), waits=waits, inc=inc)

    def wait(self, eng, *toks):
        self.op(eng, None, waits=toks)



def emit_p0(p, sb, dr, out):
    sd = [p.sem("p0d0", dma=True), p.sem("p0d1", dma=True)]
    sg = [p.sem("p0g0", dma=True), p.sem("p0g1", dma=True)]
    sv = p.sem("p0v")
    ss = p.sem("p0s")
    NT = TEXT // 128
    slot = [None, None]
    last_tmpa_reader = None
    last_stat_v = None
    st_all = sb["stats"]
    for tt in range(NT):
        bi = tt % 2
        own = tt >= NT // 2
        xt = sb["xt"][:, bi]
        st = st_all[:, bi * 24:(bi + 1) * 24]
        pv = slot[bi]
        tload = p.dma("sync", xt, dr["x_ext"][tt * 128:(tt + 1) * 128, :],
                      waits=([pv["t_xn"], pv["t_sum"]] if pv else []), inc=sd[bi])
        t_sq = p.act(sb["tmpa"][:], xt, AF.Square, accum=st[:, 0:1],
                     waits=[tload, last_tmpa_reader,
                            pv["t_stat"] if pv else None], inc=ss)
        t_sum = p.red(st[:, 1:2], xt,
                      waits=[tload] + ([pv["stores"], pv["t_xn"]] if pv else []),
                      inc=sv)
        p.ts(st[:, 2:3], st[:, 1:2], 1.0 / D)
        p.ts(st[:, 7:8], st[:, 0:1], 1.0 / D, waits=[t_sq])
        p.tt(st[:, 8:9], st[:, 2:3], st[:, 2:3], ALU.mult)
        p.tt(st[:, 9:10], st[:, 7:8], st[:, 8:9], ALU.subtract)
        t_u = p.ts(st[:, 10:11], st[:, 9:10], EPS_LN, op0=ALU.add, inc=sv)
        t_s0 = p.act(st[:, 12:13], st[:, 10:11], AF.Sqrt, waits=[t_u], inc=ss)
        p.recip(st[:, 3:4], st[:, 12:13], waits=[t_s0])
        for _ in range(2):
            p.tt(st[:, 13:14], st[:, 3:4], st[:, 3:4], ALU.mult)
            p.tt(st[:, 13:14], st[:, 13:14], st[:, 10:11], ALU.mult)
            p.ts(st[:, 13:14], st[:, 13:14], -0.5, 1.5, op0=ALU.mult, op1=ALU.add)
            p.tt(st[:, 3:4], st[:, 3:4], st[:, 13:14], ALU.mult)
        p.recip(st[:, 14:15], st[:, 10:11])
        p.tt(st[:, 15:16], st[:, 9:10], st[:, 14:15], ALU.mult)
        t_q = p.ts(st[:, 16:17], st[:, 15:16], EPS_RMS, op0=ALU.add, inc=sv)
        t_s1 = p.act(st[:, 17:18], st[:, 16:17], AF.Sqrt, waits=[t_q], inc=ss)
        p.recip(st[:, 4:5], st[:, 17:18], waits=[t_s1])
        for _ in range(2):
            p.tt(st[:, 13:14], st[:, 4:5], st[:, 4:5], ALU.mult)
            p.tt(st[:, 13:14], st[:, 13:14], st[:, 16:17], ALU.mult)
            p.ts(st[:, 13:14], st[:, 13:14], -0.5, 1.5, op0=ALU.mult, op1=ALU.add)
            p.tt(st[:, 4:5], st[:, 4:5], st[:, 13:14], ALU.mult)
        p.tt(st[:, 5:6], st[:, 10:11], st[:, 3:4], ALU.mult)
        p.tt(st[:, 6:7], st[:, 2:3], st[:, 3:4], ALU.mult)
        t_stat = p.ts(st[:, 6:7], st[:, 6:7], -1.0, inc=sv)
        t_xn = p.act(sb["tmpa"][:], xt, AF.Identity, bias=st[:, 6:7], scale=st[:, 3:4],
                     waits=[t_stat], inc=ss)
        t_xr = p.ts(sb["tmpb"][:], sb["tmpa"][:], st[:, 4:5], waits=[t_xn], inc=sv)
        t_xrh = p.copy(sb["hsta"][:, bi, 0], sb["tmpb"][:],
                       waits=[pv["stores"]] if pv else [], inc=sv)
        t_xrl = p.tt(sb["lsta"][:, bi, 0], sb["tmpb"][:], sb["hsta"][:, bi, 0],
                     ALU.subtract, inc=sv)
        trow = tt * 128
        p.dma("gpsimd", dr["xr_hi"][trow:trow + 128, :], sb["hsta"][:, bi, 0],
              waits=[t_xrh], inc=sg[bi])
        tok_st = p.dma("gpsimd", dr["xr_lo"][trow:trow + 128, :], sb["lsta"][:, bi, 0],
                       waits=[t_xrl], inc=sg[bi])
        last_tmpa = t_xr
        if own:
            orow = trow - TOK
            t_xnh = p.copy(sb["hsta"][:, bi, 1], sb["tmpa"][:], inc=sv)
            t_xnl = p.tt(sb["lsta"][:, bi, 1], sb["tmpa"][:], sb["hsta"][:, bi, 1],
                         ALU.subtract, inc=sv)
            p.dma("gpsimd", dr["xn_hi"][orow:orow + 128, :], sb["hsta"][:, bi, 1],
                  waits=[t_xnh], inc=sg[bi])
            p.dma("gpsimd", dr["xn_lo"][orow:orow + 128, :], sb["lsta"][:, bi, 1],
                  waits=[t_xnl], inc=sg[bi])
            p.dma("gpsimd", dr["mrow"][orow:orow + 128, :], st[:, 2:3],
                  waits=[t_stat], inc=sg[bi])
            tok_st = p.dma("gpsimd", dr["stdrow"][orow:orow + 128, :], st[:, 5:6],
                           waits=[t_stat], inc=sg[bi])
            last_tmpa = t_xnl
        slot[bi] = {"t_xn": t_xn, "t_sum": t_sum, "t_stat": t_stat, "stores": tok_st}
        last_tmpa_reader = last_tmpa
        last_stat_v = t_stat
        yield
    out["p0"] = [sg[0].tok(), sg[1].tok(), sv.tok(), ss.tok()]


def emit_prep(p, sb, dr, marks):
    sld = p.sem("prd", dma=True)
    sxb = p.sem("prx", dma=True)
    sst = {"qkv": p.sem("prs_qkv", dma=True)}
    for e in range(NE):
        sst[f"moe{e}"] = p.sem(f"prs_m{e}", dma=True)
    for g in range(8):
        sst[f"swg{g}"] = p.sem(f"prs_s{g}", dma=True)
    for j in range(4):
        sst[f"woj{j}"] = p.sem(f"prs_o{j}", dma=True)
    ssc = p.sem("prsc")
    svc = p.sem("prvc")
    st8 = {"hi": [None, None], "lo": [None, None], "ssem": None, "n": 0}

    def chunk_T(src_ap, dst_hi, dst_lo, w):
        cb = st8["n"] % 2
        st8["n"] += 1
        ws = sb["wslab"][:, cb, :D]
        tl = p.dma("sync", ws, src_ap, waits=[st8["hi"][cb], st8["lo"][cb]], inc=sld)
        th = p.act(sb["wshi"][:, :D], ws, AF.Identity,
                   waits=[tl, (sxb, sxb.val)], inc=ssc)
        tlo = p.tt(sb["wslo"][:, :D], ws, sb["wshi"][:, :D], ALU.subtract,
                   waits=[tl, th, (sxb, sxb.val)], inc=svc)
        ssem = st8["ssem"]
        sw0 = [(ssem, ssem.val)] if ssem is not None else []
        txh = p.dma_t("sync", sb["xbout"][:, 0], sb["wshi"][:, :D],
                      waits=[th] + sw0, inc=sxb)
        txl = p.dma_t("sync", sb["xbout"][:, 1], sb["wslo"][:, :D],
                      waits=[tlo] + sw0, inc=sxb)
        p.dma("gpsimd", dst_hi, sb["xbout"][:, 0], waits=[(sxb, sxb.val)], inc=sst[w])
        tsl = p.dma("gpsimd", dst_lo, sb["xbout"][:, 1], waits=[(sxb, sxb.val)],
                    inc=sst[w])
        st8["hi"][cb] = th
        st8["lo"][cb] = tlo
        st8["ssem"] = sst[w]
        marks[w] = tsl

    def chunk_N(src_ap, dst_hi, dst_lo, w, ncols):
        cb = st8["n"] % 2
        st8["n"] += 1
        ws = sb["wslab"][:, cb, :ncols]
        tl = p.dma("sync", ws, src_ap, waits=[st8["hi"][cb], st8["lo"][cb]], inc=sld)
        ssem = st8["ssem"]
        sw0 = [(ssem, ssem.val)] if ssem is not None else []
        th = p.act(sb["wshi"][:, :ncols], ws, AF.Identity,
                   waits=[tl] + sw0, inc=ssc)
        tlo = p.tt(sb["wslo"][:, :ncols], ws, sb["wshi"][:, :ncols], ALU.subtract,
                   waits=[tl, th] + sw0, inc=svc)
        p.dma("gpsimd", dst_hi, sb["wshi"][:, :ncols], waits=[th], inc=sst[w])
        tsl = p.dma("gpsimd", dst_lo, sb["wslo"][:, :ncols], waits=[tlo], inc=sst[w])
        st8["hi"][cb] = th
        st8["lo"][cb] = tlo
        st8["ssem"] = sst[w]
        marks[w] = tsl

    for fc in range(48):
        chunk_T(dr["w_qkv"][fc * 128:(fc + 1) * 128, :],
                dr["qkvT_hi"][fc], dr["qkvT_lo"][fc], "qkv")
        yield
    for e in range(NE):
        for rc in range(DC):
            rows = slice(rc * 128, (rc + 1) * 128)
            chunk_N(dr["w_gate"][e][rows, :], dr["gateN_hi"][e][rows, :],
                    dr["gateN_lo"][e][rows, :], f"moe{e}", FMOE)
            chunk_N(dr["w_up"][e][rows, :], dr["upN_hi"][e][rows, :],
                    dr["upN_lo"][e][rows, :], f"moe{e}", FMOE)
            yield
        for rc in range(6):
            rows = slice(rc * 128, (rc + 1) * 128)
            chunk_N(dr["w_down"][e][rows, :], dr["downN_hi"][e][rows, :],
                    dr["downN_lo"][e][rows, :], f"moe{e}", D)
            yield
        marks[f"moe{e}done"] = True
    for fp in range(64):
        for rc in (fp, 64 + fp):
            rows = slice(rc * 128, (rc + 1) * 128)
            chunk_N(dr["w_sw"][rows, :], dr["swN_hi"][rows, :],
                    dr["swN_lo"][rows, :], f"swg{fp // 8}", D)
        if fp % 8 == 7:
            marks[f"swgdone{fp // 8}"] = marks[f"swg{fp // 8}"]
        yield
    for j in range(4):
        cols = slice(j * 2048, (j + 1) * 2048)
        for rc in range(DC):
            rows = slice(rc * 128, (rc + 1) * 128)
            chunk_N(dr["w_out"][rows, cols], dr["woN_hi"][rows, cols],
                    dr["woN_lo"][rows, cols], f"woj{j}", 2048)
            yield
        marks[f"wojdone{j}"] = marks[f"woj{j}"]


def emit_p1(p, sb, dr, p0_toks, marks, out):
    sxr = p.sem("p1xr", dma=True)
    smk = p.sem("p1mk", dma=True)
    swt = [p.sem(f"p1w{i}", dma=True) for i in range(2)]
    srow = p.sem("p1row", dma=True)
    syst = [p.sem(f"p1y{i}", dma=True) for i in range(2)]
    stm = p.sem("p1t")
    sv = p.sem("p1v")
    ss = p.sem("p1s")

    p.memset(sb["carry"][:], 0.0)
    p.memset(sb["ones1"][:], 1.0)
    p.memset(sb["onesT"][:], 1.0, inc=sv)
    t_init = sv.tok()
    t_mask = p.dma("sync", sb["maskc"][:], dr["maskh"], inc=smk)

    state = {
        "wt_use": [None, None],
        "ps_free": [None, None],
        "sq_free": None,
        "pssq_free": None,
        "sc_scalar": None,
        "xrs_use": None,
    }
    qkv_ready = marks["qkv"]

    def mm_sweep(base_chunk, ssq):
        ev = []
        pend_ssq = None
        for i in range(DC):
            bi = i % 2
            tlh = p.dma("sync", sb["wtile"][:, bi, 0], dr["qkvT_hi"][base_chunk + i],
                        waits=[state["wt_use"][bi], qkv_ready]
                        if i < 2 else [state["wt_use"][bi]], inc=swt[bi])
            tll = p.dma("sync", sb["wtile"][:, bi, 1], dr["qkvT_lo"][base_chunk + i],
                        inc=swt[bi])
            pb = sb[f"pA{i % 2}"]
            w0 = [(swt[bi], swt[bi].val), (sxr, sxr.val), state["ps_free"][i % 2]]
            for dc in range(DC):
                p.mm(pb[:], sb["wtile"][:, bi, 0, dc], sb["xrs_hi"][:, dc],
                     start=(dc == 0), stop=False, waits=w0 if dc == 0 else [])
                p.mm(pb[:], sb["wtile"][:, bi, 0, dc], sb["xrs_lo"][:, dc],
                     start=False, stop=False)
                p.mm(pb[:], sb["wtile"][:, bi, 1, dc], sb["xrs_hi"][:, dc],
                     start=False, stop=(dc == DC - 1),
                     inc=stm if dc == DC - 1 else None)
            t_mm = stm.tok()
            state["wt_use"][bi] = t_mm
            state["xrs_use"] = t_mm
            if pend_ssq is not None:
                pi, t_sq = pend_ssq
                t_ssq = p.mm(sb["pA3"][0:1, :], sb["ones1"][:], sb["sq512"][:],
                             start=(pi == 0), stop=(pi == DC - 1),
                             waits=[t_sq, t_init, state["pssq_free"]]
                             if pi == 0 else [t_sq], inc=stm)
                state["sq_free"] = t_ssq
            t_ev = p.copy(sb["Sbuf"][:, i], pb[:], waits=[t_mm], inc=sv)
            ev.append(t_ev)
            state["ps_free"][i % 2] = t_ev
            if ssq:
                t_sq = p.act(sb["sq512"][:], sb["Sbuf"][:, i], AF.Square,
                             waits=[t_ev, state["sq_free"]], inc=ss)
                pend_ssq = (i, t_sq)
            yield
        if ssq and pend_ssq is not None:
            pi, t_sq = pend_ssq
            t_ssq = p.mm(sb["pA3"][0:1, :], sb["ones1"][:], sb["sq512"][:],
                         start=(pi == 0), stop=(pi == DC - 1),
                         waits=[t_sq], inc=stm)
            state["sq_free"] = t_ssq
        return ev

    def rsqrt_bc(bc, rowslot, tsl):
        t_ev = p.copy(sb["sc512d"][0:1, :], sb["pA3"][0:1, :],
                      waits=[state["sq_free"], state["sc_scalar"]], inc=sv)
        state["pssq_free"] = t_ev
        p.dma("gpsimd", dr["rows"][rowslot:rowslot + 1, tsl:tsl + 512],
              sb["sc512d"][0:1, :], waits=[t_ev], inc=srow)
        t_bc = p.dma("gpsimd", sb["sc512a"][:],
                     dr["rows"][rowslot, tsl:tsl + 512].partition_broadcast(128),
                     waits=[(srow, srow.val)], inc=srow)
        t_u = p.ts(sb["sc512a"][:], sb["sc512a"][:], 1.0 / D, EPS_RMS,
                   op0=ALU.mult, op1=ALU.add, waits=[t_bc], inc=sv)
        t_s0 = p.act(sb["sc512c"][:], sb["sc512a"][:], AF.Sqrt, waits=[t_u], inc=ss)
        state["sc_scalar"] = t_s0
        p.recip(bc, sb["sc512c"][:], waits=[t_s0])
        for _ in range(2):
            p.tt(sb["sc512b"][:], bc, bc, ALU.mult)
            p.tt(sb["sc512b"][:], sb["sc512b"][:], sb["sc512a"][:], ALU.mult)
            p.ts(sb["sc512b"][:], sb["sc512b"][:], -0.5, 1.5, op0=ALU.mult, op1=ALU.add)
            p.tt(bc, bc, sb["sc512b"][:], ALU.mult)
        return p.copy(sb["sc512b"][0:1, 0:1], bc[0:1, 0:1], inc=sv)

    for sl in range(4):
        own = sl >= 2
        tsl = sl * 512
        t_xh = p.dma_t("sync", sb["xrs_hi"][:],
                       dr["xr_hi"][tsl:tsl + 512, :],
                       waits=[state["xrs_use"]] + (list(p0_toks) if sl == 0 else []),
                       inc=sxr)
        t_xl = p.dma_t("sync", sb["xrs_lo"][:], dr["xr_lo"][tsl:tsl + 512, :],
                       inc=sxr)

        for _ in mm_sweep(16, ssq=True):
            yield
        t_rk = rsqrt_bc(sb["bc0"][:, :512], 0, tsl)
        yield
        w_tok = []
        for i in range(DC):
            tb = sb["sc512c"] if i % 2 == 0 else sb["sc512d"]
            t1 = p.tt(tb[:], sb["Sbuf"][:, i], sb["bc0"][:, :512], ALU.mult,
                      waits=[w_tok[i - 2]] if i >= 2 else [t_rk], inc=sv)
            t_w = p.act(sb["wbuf"][:, i], tb[:], AF.Exp, waits=[t1], inc=ss)
            state["sc_scalar"] = t_w
            w_tok.append(t_w)
            if i % 4 == 3:
                yield

        wv_tok = []
        for i in range(DC):
            bi = i % 2
            tlh = p.dma("sync", sb["wtile"][:, bi, 0], dr["qkvT_hi"][32 + i],
                        waits=[state["wt_use"][bi]], inc=swt[bi])
            p.dma("sync", sb["wtile"][:, bi, 1], dr["qkvT_lo"][32 + i], inc=swt[bi])
            pb = sb[f"pA{i % 2}"]
            w0 = [(swt[bi], swt[bi].val), state["ps_free"][i % 2]]
            for dc in range(DC):
                p.mm(pb[:], sb["wtile"][:, bi, 0, dc], sb["xrs_hi"][:, dc],
                     start=(dc == 0), stop=False, waits=w0 if dc == 0 else [])
                p.mm(pb[:], sb["wtile"][:, bi, 0, dc], sb["xrs_lo"][:, dc],
                     start=False, stop=False)
                p.mm(pb[:], sb["wtile"][:, bi, 1, dc], sb["xrs_hi"][:, dc],
                     start=False, stop=(dc == DC - 1),
                     inc=stm if dc == DC - 1 else None)
            t_mm = stm.tok()
            state["wt_use"][bi] = t_mm
            state["xrs_use"] = t_mm
            t_wv = p.tt(sb["wvbuf"][:, i], sb["wbuf"][:, i], pb[:], ALU.mult,
                        waits=[t_mm, w_tok[i]], inc=sv)
            wv_tok.append(t_wv)
            state["ps_free"][i % 2] = t_wv
            yield

        if not own:
            for i in range(DC):
                c0, c1 = 4 * (i % 2), 4 * (i % 2) + 1
                p.red(sb["ctmp"][:, c0:c0 + 1], sb["wbuf"][:, i],
                      waits=[w_tok[i], (smk, smk.val)])
                p.tt(sb["ctmp"][:, c1:c1 + 2 - 1], sb["ctmp"][:, c0:c0 + 1],
                     sb["maskc"][:], ALU.mult)
                p.tt(sb["carry"][:, 0, i:i + 1], sb["carry"][:, 0, i:i + 1],
                     sb["ctmp"][:, c1:c1 + 1], ALU.add)
                p.red(sb["ctmp"][:, c0 + 2:c0 + 3], sb["wvbuf"][:, i],
                      waits=[wv_tok[i]])
                p.tt(sb["ctmp"][:, c1 + 2:c1 + 3], sb["ctmp"][:, c0 + 2:c0 + 3],
                     sb["maskc"][:], ALU.mult)
                p.tt(sb["carry"][:, 1, i:i + 1], sb["carry"][:, 1, i:i + 1],
                     sb["ctmp"][:, c1 + 2:c1 + 3], ALU.add)
                if i % 4 == 3:
                    yield
            continue

        for _ in mm_sweep(0, ssq=True):
            yield
        t_rq = rsqrt_bc(sb["bc1"][:, :512], 1, tsl)
        yield

        off = (sl - 2) * 512
        for i in range(DC):
            rot = i % 2
            p.scan(sb["sc512a"][:], sb["onesT"][:, :512], sb["wbuf"][:, i],
                   sb["carry"][:, 0, i:i + 1], ALU.mult, ALU.add,
                   waits=[w_tok[i]] + ([state["sc_scalar"]] if i == 0 else []))
            p.copy(sb["carry"][:, 0, i:i + 1], sb["sc512a"][:, 511:512])
            p.scan(sb["sc512b"][:], sb["onesT"][:, :512], sb["wvbuf"][:, i],
                   sb["carry"][:, 1, i:i + 1], ALU.mult, ALU.add)
            p.copy(sb["carry"][:, 1, i:i + 1], sb["sc512b"][:, 511:512])
            p.ts(sb["sc512c"][:], sb["sc512a"][:], EPS_WC, op0=ALU.add)
            p.recip(sb["sc512c"][:], sb["sc512c"][:])
            t_qs = p.tt(sb["sc512d"][:], sb["Sbuf"][:, i], sb["bc1"][:, :512], ALU.mult,
                        waits=[t_rq], inc=sv)
            t_sig = p.act(sb["sq512"][:], sb["sc512d"][:], AF.Sigmoid,
                          waits=[t_qs, state["sq_free"]], inc=ss)
            state["sc_scalar"] = t_sig
            p.tt(sb["sc512d"][:], sb["sq512"][:], sb["sc512b"][:], ALU.mult,
                 waits=[t_sig])
            p.tt(sb["sc512d"][:], sb["sc512d"][:], sb["sc512c"][:], ALU.mult)
            t_h = p.copy(sb["yst"][:, rot, 0], sb["sc512d"][:],
                         waits=[(syst[rot], syst[rot].val)], inc=sv)
            t_l = p.tt(sb["yst"][:, rot, 1], sb["sc512d"][:], sb["yst"][:, rot, 0],
                       ALU.subtract, inc=sv)
            p.dma("gpsimd", dr["y_hi"][i][:, off:off + 512], sb["yst"][:, rot, 0],
                  waits=[t_h], inc=syst[rot])
            p.dma("gpsimd", dr["y_lo"][i][:, off:off + 512], sb["yst"][:, rot, 1],
                  waits=[t_l], inc=syst[rot])
            yield

    out["p1"] = [syst[0].tok(), syst[1].tok(), sv.tok(), ss.tok(), stm.tok(),
                 (srow, srow.val)]


def emit_p4(p, sb, dr, p1_toks, gprep, marks, out):
    sy = p.sem("p4y", dma=True)
    swl = [p.sem(f"p4w{i}", dma=True) for i in range(2)]
    shs = [p.sem(f"p4h{i}", dma=True) for i in range(2)]
    stm = p.sem("p4t")
    sv = p.sem("p4v")
    ss = p.sem("p4s")

    t_yh = p.dma("sync", sb["ybuf_hi"][:], dr["y_hi"].rearrange("c p t -> p c t"),
                 waits=list(p1_toks), inc=sy)
    t_yl = p.dma("sync", sb["ybuf_lo"][:], dr["y_lo"].rearrange("c p t -> p c t"),
                 inc=sy)
    wt_use = [None, None]
    ps_free = [None, None]
    silug_use = None
    hst_use = [None, None]
    t_first = None

    for fp in range(64):
        bi = fp % 2
        ur = slice(fp * 128, (fp + 1) * 128)
        gr = slice(DFF + fp * 128, DFF + (fp + 1) * 128)
        while f"swgdone{fp // 8}" not in marks:
            next(gprep)
        tw = p.dma_t("sync", sb["swtile"][:, bi, 0, 0], dr["swN_hi"][ur, :],
                     waits=[wt_use[bi], marks[f"swgdone{fp // 8}"]], inc=swl[bi])
        p.dma_t("sync", sb["swtile"][:, bi, 1, 0], dr["swN_lo"][ur, :], inc=swl[bi])
        p.dma_t("sync", sb["swtile"][:, bi, 0, 1], dr["swN_hi"][gr, :], inc=swl[bi])
        p.dma_t("sync", sb["swtile"][:, bi, 1, 1], dr["swN_lo"][gr, :], inc=swl[bi])
        for ug in range(2):
            pb = sb["pB0"] if ug == 0 else sb["pB1"]
            w0 = [(swl[bi], swl[bi].val), (sy, sy.val), ps_free[ug]]
            for dc in range(DC):
                for th in range(2):
                    hs = slice(th * 512, (th + 1) * 512)
                    p.mm(pb[:, hs], sb["swtile"][:, bi, 0, ug, dc],
                         sb["ybuf_hi"][:, dc, hs], start=(dc == 0), stop=False,
                         waits=w0 if dc == 0 and th == 0 else [])
                    p.mm(pb[:, hs], sb["swtile"][:, bi, 0, ug, dc],
                         sb["ybuf_lo"][:, dc, hs], start=False, stop=False)
                    p.mm(pb[:, hs], sb["swtile"][:, bi, 1, ug, dc],
                         sb["ybuf_hi"][:, dc, hs], start=False, stop=(dc == DC - 1),
                         inc=stm if (dc == DC - 1 and th == 1) else None)
            if ug == 0:
                t_u = stm.tok()
            else:
                t_g = stm.tok()
        wt_use[bi] = t_g
        t_sg = p.act(sb["silug"][:], sb["pB1"][:], AF.Silu,
                     waits=[t_g, silug_use], inc=ss)
        ps_free[1] = t_sg
        t_h = p.tt(sb["ht1"][:], sb["pB0"][:], sb["silug"][:], ALU.mult,
                   waits=[t_u, t_sg], inc=sv)
        ps_free[0] = t_h
        silug_use = t_h
        rot = fp % 2
        t_hh = p.copy(sb["hst"][:, rot, 0], sb["ht1"][:],
                      waits=[(shs[rot], shs[rot].val)], inc=sv)
        t_hl = p.tt(sb["hst"][:, rot, 1], sb["ht1"][:], sb["hst"][:, rot, 0],
                    ALU.subtract, inc=sv)
        p.dma("gpsimd", dr["h_hi"][fp], sb["hst"][:, rot, 0], waits=[t_hh],
              inc=shs[rot])
        p.dma("gpsimd", dr["h_lo"][fp], sb["hst"][:, rot, 1], waits=[t_hl],
              inc=shs[rot])
        yield

    out["p4"] = [shs[0].tok(), shs[1].tok(), sv.tok(), ss.tok(), stm.tok()]


def emit_p5(p, sb, dr, p4_toks, gprep, marks, out):
    sh = [p.sem(f"p5h{i}", dma=True) for i in range(2)]
    swl = [p.sem(f"p5w{i}", dma=True) for i in range(2)]
    sxn = [p.sem(f"p5x{i}", dma=True) for i in range(2)]
    sss = [p.sem(f"p5s{i}", dma=True) for i in range(2)]
    stm = p.sem("p5t")
    sv = p.sem("p5v")

    hs_use = [None, None]
    wt_use = [None, None]
    ps_free = [None, None, None, None]

    for hsuper in range(8):
        while f"wojdone{hsuper // 2}" not in marks:
            next(gprep)
        hb = hsuper % 2
        hrows = slice(8 * hsuper, 8 * hsuper + 8)
        t_h1 = p.dma("sync", sb["h5_hi"][:, hb],
                     dr["h_hi"].rearrange("c p t -> p c t")[:, hrows, :],
                     waits=[hs_use[hb]] + (list(p4_toks) if hsuper < 2 else []),
                     inc=sh[hb])
        p.dma("sync", sb["h5_lo"][:, hb],
              dr["h_lo"].rearrange("c p t -> p c t")[:, hrows, :], inc=sh[hb])
        for dout in range(DC):
            wb = dout % 2
            rows = slice(dout * 128, (dout + 1) * 128)
            cols = slice(hsuper * 1024, (hsuper + 1) * 1024)
            p.dma_t("sync", sb["wotile"][:, wb, 0], dr["woN_hi"][rows, cols],
                    waits=[wt_use[wb]] +
                    ([marks[f"wojdone{hsuper // 2}"]] if dout < 2 else []),
                    inc=swl[wb])
            p.dma_t("sync", sb["wotile"][:, wb, 1], dr["woN_lo"][rows, cols],
                    inc=swl[wb])
            for tb in range(2):
                bank = (dout % 2) * 2 + tb
                pb = sb[f"pA{bank}"]
                ts_ = slice(tb * 512, (tb + 1) * 512)
                w0 = [(swl[wb], swl[wb].val), (sh[hb], sh[hb].val), ps_free[bank]]
                for hc in range(8):
                    p.mm(pb[:], sb["wotile"][:, wb, 0, hc],
                         sb["h5_hi"][:, hb, hc, ts_], start=(hc == 0), stop=False,
                         waits=w0 if hc == 0 else [])
                    p.mm(pb[:], sb["wotile"][:, wb, 0, hc],
                         sb["h5_lo"][:, hb, hc, ts_], start=False, stop=False)
                    p.mm(pb[:], sb["wotile"][:, wb, 1, hc],
                         sb["h5_hi"][:, hb, hc, ts_], start=False, stop=(hc == 7),
                         inc=stm if hc == 7 else None)
                t_mm = stm.tok()
                wt_use[wb] = t_mm
                hs_use[hb] = t_mm
                if hsuper == 0:
                    t_acc = p.copy(sb["oacc"][:, dout, ts_], pb[:],
                                   waits=[t_mm], inc=sv)
                else:
                    t_acc = p.tt(sb["oacc"][:, dout, ts_], sb["oacc"][:, dout, ts_],
                                 pb[:], ALU.add, waits=[t_mm], inc=sv)
                ps_free[bank] = t_acc
            yield

    xn_use = [None, None]
    sst_use = [None, None]
    for dout in range(DC):
        bi = dout % 2
        p.dma_t("sync", sb["xn5h"][:, bi], dr["xn_hi"][:, dout * 128:(dout + 1) * 128],
                waits=[xn_use[bi]], inc=sxn[bi])
        p.dma_t("sync", sb["xn5l"][:, bi], dr["xn_lo"][:, dout * 128:(dout + 1) * 128],
                inc=sxn[bi])
        t_1 = p.tt(sb["sfin"][:, bi], sb["xn5h"][:, bi], sb["xn5l"][:, bi], ALU.add,
                   waits=[(sxn[bi], sxn[bi].val), (sss[bi], sss[bi].val)], inc=sv)
        xn_use[bi] = t_1
        t_s = p.tt(sb["sfin"][:, bi], sb["sfin"][:, bi], sb["oacc"][:, dout, :],
                   ALU.add, inc=sv)
        p.dma("gpsimd", dr["s_fm"][dout], sb["sfin"][:, bi], waits=[t_s], inc=sss[bi])
        yield

    out["p5"] = [sss[0].tok(), sss[1].tok(), sv.tok(), stm.tok(),
                 (sxn[0], sxn[0].val), (sxn[1], sxn[1].val)]


def emit_p3(p, sb, dr, prev_toks, gprep, marks, out):
    sxn = p.sem("p3xn", dma=True)
    swr = p.sem("p3wr", dma=True)
    sgu = [p.sem(f"p3g{i}", dma=True) for i in range(2)]
    sdt = [p.sem("p3dt0", dma=True), p.sem("p3dt1", dma=True)]
    sce = p.sem("p3ce", dma=True)
    scf = p.sem("p3cf", dma=True)
    sma = [p.sem(f"p3m{i}", dma=True) for i in range(2)]
    sms = [p.sem(f"p3n{i}", dma=True) for i in range(2)]
    stm = p.sem("p3t")
    sv = p.sem("p3v")
    ss = p.sem("p3s")

    t_x1 = p.dma_t("sync", sb["xnf3h"][:], dr["xn_hi"],
                   waits=list(prev_toks), inc=sxn)
    t_x2 = p.dma_t("sync", sb["xnf3l"][:], dr["xn_lo"], inc=sxn)
    t_wr = p.dma("sync", sb["wrs"][:], dr["w_r"].rearrange("(c p) e -> p c e", p=128),
                 inc=swr)
    t_eye = p.dma("sync", sb["eyes"][:], dr["eye128"], inc=swr)
    t_io = p.dma("sync", sb["iotas"][:], dr["iota16"], inc=swr)
    t_wrh = p.act(sb["wrh"][:], sb["wrs"][:], AF.Identity,
                  waits=[(swr, swr.val)], inc=ss)
    t_wrl = p.tt(sb["wrl"][:], sb["wrs"][:], sb["wrh"][:], ALU.subtract,
                 waits=[t_wrh], inc=sv)
    yield

    ctile_use = None
    pA3_use = None
    for tt in range(8):
        pb = sb["pA2"]
        w0 = [(sxn, sxn.val), t_wrl, ctile_use]
        for dc in range(DC):
            l_hi = sb["xnf3h"][:, dc, tt * 128:(tt + 1) * 128]
            l_lo = sb["xnf3l"][:, dc, tt * 128:(tt + 1) * 128]
            p.mm(pb[:, :NE], l_hi, sb["wrh"][:, dc], start=(dc == 0), stop=False,
                 waits=w0 if dc == 0 else [])
            p.mm(pb[:, :NE], l_lo, sb["wrh"][:, dc], start=False, stop=False)
            p.mm(pb[:, :NE], l_hi, sb["wrl"][:, dc], start=False, stop=(dc == DC - 1),
                 inc=stm if dc == DC - 1 else None)
        t_mm = stm.tok()
        q = sb["rtq"]
        sc = q[:, 0:16]
        t_sc = p.copy(sc, pb[:, :NE], waits=[t_mm], inc=sv)
        p.red(q[:, 16:17], sc, alu=ALU.max)
        p.ts(q[:, 17:18], q[:, 16:17], -1.0)
        t_sub = p.ts(q[:, 20:36], sc, q[:, 17:18], op0=ALU.add, inc=sv)
        t_ex = p.act(q[:, 20:36], q[:, 20:36], AF.Exp, waits=[t_sub], inc=ss)
        t_sm = p.red(q[:, 18:19], q[:, 20:36], waits=[t_ex], inc=sv)
        p.recip(q[:, 19:20], q[:, 18:19])
        p.ts(q[:, 20:36], q[:, 20:36], q[:, 19:20], op0=ALU.mult)
        p.red(q[:, 36:37], q[:, 20:36], alu=ALU.max)
        p.ts(q[:, 40:56], q[:, 20:36], q[:, 36:37], op0=ALU.is_ge)
        p.ts(q[:, 40:56], q[:, 40:56], -1e6, 1e6, op0=ALU.mult, op1=ALU.add)
        p.tt(q[:, 40:56], q[:, 40:56], sb["iotas"][:], ALU.add)
        p.red(q[:, 37:38], q[:, 40:56], alu=ALU.min)
        p.ts(q[:, 40:56], sb["iotas"][:], q[:, 37:38], op0=ALU.is_equal)
        p.tt(q[:, 56:72], q[:, 40:56], q[:, 20:36], ALU.mult)
        p.tt(q[:, 56:72], q[:, 20:36], q[:, 56:72], ALU.subtract)
        p.red(q[:, 38:39], q[:, 56:72], alu=ALU.max)
        p.tt(q[:, 39:40], q[:, 36:37], q[:, 38:39], ALU.add)
        p.recip(q[:, 39:40], q[:, 39:40])
        p.ts(q[:, 56:72], q[:, 56:72], q[:, 38:39], op0=ALU.is_ge)
        p.ts(q[:, 56:72], q[:, 56:72], -1e6, 1e6, op0=ALU.mult, op1=ALU.add)
        p.tt(q[:, 56:72], q[:, 56:72], sb["iotas"][:], ALU.add)
        p.red(q[:, 37:38], q[:, 56:72], alu=ALU.min)
        p.ts(q[:, 56:72], sb["iotas"][:], q[:, 37:38], op0=ALU.is_equal)
        p.tt(q[:, 72:73], q[:, 36:37], q[:, 39:40], ALU.mult)
        p.tt(q[:, 73:74], q[:, 38:39], q[:, 39:40], ALU.mult)
        p.ts(q[:, 40:56], q[:, 40:56], q[:, 72:73], op0=ALU.mult)
        p.ts(q[:, 56:72], q[:, 56:72], q[:, 73:74], op0=ALU.mult)
        t_c = p.tt(q[:, 40:56], q[:, 40:56], q[:, 56:72], ALU.add, inc=sv)
        t_tr = p.op("tensor",
                    lambda eh, o=sb["pA3"][0:NE, 0:128], i=q[:, 40:56],
                    e=sb["eyes"][:]: eh.transpose(o, i, e),
                    waits=[t_c, (swr, swr.val), pA3_use], inc=stm)
        t_cf = p.copy(sb["cfm"][0:NE, tt * 128:(tt + 1) * 128],
                      sb["pA3"][0:NE, 0:128], waits=[t_tr], inc=sv)
        ctile_use = t_cf
        pA3_use = t_cf
        last_sc = t_sc
        yield
    p.dma("gpsimd", dr["c_fm"], sb["cfm"][0:NE, :], waits=[ctile_use], inc=scf)

    gu_par_use = [None, None]
    dt_use2 = [None, None]
    ps_freeA = [last_sc, last_sc, last_sc, pA3_use]
    ps_freeB = [None, None]
    ce_use = None
    htil_use = None
    macc_use = [None, None]
    silug_use = [None, None]
    chain_rot = 0
    fch_ctr = 0
    def gu_loads(e2, fch2):
        par2 = (e2 * 3 + fch2) % 2
        cs = slice(fch2 * 256, (fch2 + 1) * 256)
        me = marks[f"moe{e2}"]
        p.dma("sync", sb["guh"][:, par2, 0],
              dr["gateN_hi"][e2][:, cs].rearrange("(c p) f -> p c f", p=128),
              waits=[gu_par_use[par2], me], inc=sgu[par2])
        p.dma("sync", sb["gul"][:, par2, 0],
              dr["gateN_lo"][e2][:, cs].rearrange("(c p) f -> p c f", p=128),
              inc=sgu[par2])
        p.dma("sync", sb["guh"][:, par2, 1],
              dr["upN_hi"][e2][:, cs].rearrange("(c p) f -> p c f", p=128),
              inc=sgu[par2])
        p.dma("sync", sb["gul"][:, par2, 1],
              dr["upN_lo"][e2][:, cs].rearrange("(c p) f -> p c f", p=128),
              inc=sgu[par2])
        tsp[par2] = (sgu[par2], sgu[par2].val)

    tsp = {}
    for e in range(NE):
        while f"moe{e}done" not in marks:
            next(gprep)
        t_t2 = None
        t_ce = p.dma("gpsimd", sb["cebc"][:],
                     dr["c_fm"][e, :].partition_broadcast(128),
                     waits=[ce_use, (scf, scf.val)], inc=sce)
        for fch in range(3):
            par = (e * 3 + fch) % 2
            fch_ctr += 1
            if (e, fch) == (0, 0):
                gu_loads(0, 0)
            ne, nf = (e, fch + 1) if fch < 2 else (e + 1, 0)
            if ne < NE:
                if nf == 0:
                    while f"moe{ne}done" not in marks:
                        next(gprep)
                gu_loads(ne, nf)
            t_split = tsp[par]
            for fc3 in range(2):
              fc = fch * 2 + fc3
              fs = slice(fc3 * 128, (fc3 + 1) * 128)
              for th in range(2):
                hs = slice(th * 512, (th + 1) * 512)
                pg = sb[f"pA{2 * th}"]
                pu = sb[f"pA{2 * th + 1}"]
                w0 = [t_split, (sxn, sxn.val), ps_freeA[2 * th],
                      ps_freeA[2 * th + 1]]
                for dc in range(DC):
                    p.mm(pg[:], sb["guh"][:, par, 0, dc, fs], sb["xnf3h"][:, dc, hs],
                         start=(dc == 0), stop=False, waits=w0 if dc == 0 else [])
                    p.mm(pg[:], sb["guh"][:, par, 0, dc, fs], sb["xnf3l"][:, dc, hs],
                         start=False, stop=False)
                    p.mm(pg[:], sb["gul"][:, par, 0, dc, fs], sb["xnf3h"][:, dc, hs],
                         start=False, stop=(dc == DC - 1))
                    p.mm(pu[:], sb["guh"][:, par, 1, dc, fs], sb["xnf3h"][:, dc, hs],
                         start=(dc == 0), stop=False)
                    p.mm(pu[:], sb["guh"][:, par, 1, dc, fs], sb["xnf3l"][:, dc, hs],
                         start=False, stop=False)
                    p.mm(pu[:], sb["gul"][:, par, 1, dc, fs], sb["xnf3h"][:, dc, hs],
                         start=False, stop=(dc == DC - 1),
                         inc=stm if dc == DC - 1 else None)
                t_mm = stm.tok()
                gu_par_use[par] = t_mm
                cr = chain_rot % 2
                chain_rot += 1
                csl = slice(cr * 512, (cr + 1) * 512)
                t_sg = p.act(sb["silug"][:, csl], pg[:], AF.Silu,
                             waits=[t_mm, silug_use[cr]], inc=ss)
                t_t1 = p.tt(sb["mt1"][:, csl], pu[:], sb["silug"][:, csl],
                            ALU.mult, waits=[t_sg, (sce, sce.val)], inc=sv)
                ps_freeA[2 * th] = t_t1
                ps_freeA[2 * th + 1] = t_t1
                silug_use[cr] = t_t1
                t_t2 = p.tt(sb["mt1"][:, csl], sb["mt1"][:, csl],
                            sb["cebc"][:, hs], ALU.mult, inc=sv)
                t_hh = p.copy(sb["htil_hi"][:, fc, hs], sb["mt1"][:, csl],
                              waits=[htil_use] if (fch, fc3, th) == (0, 0, 0) else [],
                              inc=sv)
                p.tt(sb["htil_lo"][:, fc, hs], sb["mt1"][:, csl],
                     sb["htil_hi"][:, fc, hs], ALU.subtract, inc=sv)
            yield
        ce_use = t_t2
        t_hready = sv.tok()

        for dgc in range(DC):
            db = dgc % 2
            p.dma("sync", sb["dth"][:, db],
                  dr["downN_hi"][e].rearrange("(c p) d -> p c d", p=128)
                  [:, :, dgc * 128:(dgc + 1) * 128],
                  waits=[dt_use2[db]], inc=sdt[db])
            p.dma("sync", sb["dtl"][:, db],
                  dr["downN_lo"][e].rearrange("(c p) d -> p c d", p=128)
                  [:, :, dgc * 128:(dgc + 1) * 128], inc=sdt[db])
            tl_d = (sdt[db], sdt[db].val)
            pb = sb[f"pB{dgc % 2}"]
            mrot = dgc % 2
            if e > 0:
                p.dma("gpsimd", sb["macc"][:, mrot], dr["moe_fm"][dgc],
                      waits=[macc_use[mrot], (sms[mrot], sms[mrot].val)],
                      inc=sma[mrot])
            w0 = [tl_d, t_hready, ps_freeB[dgc % 2]]
            for fc in range(6):
                for th in range(2):
                    hs = slice(th * 512, (th + 1) * 512)
                    p.mm(pb[:, hs], sb["dth"][:, db, fc], sb["htil_hi"][:, fc, hs],
                         start=(fc == 0), stop=False, waits=w0 if fc == 0 else [])
                    p.mm(pb[:, hs], sb["dth"][:, db, fc], sb["htil_lo"][:, fc, hs],
                         start=False, stop=False)
                    p.mm(pb[:, hs], sb["dtl"][:, db, fc], sb["htil_hi"][:, fc, hs],
                         start=False, stop=(fc == 5),
                         inc=stm if (fc, th) == (5, 1) else None)
            t_mm = stm.tok()
            dt_use2[db] = t_mm
            htil_use = t_mm
            if e == 0:
                t_acc = p.copy(sb["macc"][:, mrot], pb[:],
                               waits=[t_mm, (sms[mrot], sms[mrot].val),
                                      macc_use[mrot]], inc=sv)
            else:
                t_acc = p.tt(sb["macc"][:, mrot], sb["macc"][:, mrot], pb[:],
                             ALU.add, waits=[t_mm, (sma[mrot], sma[mrot].val)],
                             inc=sv)
            ps_freeB[dgc % 2] = t_acc
            macc_use[mrot] = t_acc
            p.dma("gpsimd", dr["moe_fm"][dgc], sb["macc"][:, mrot],
                  waits=[t_acc], inc=sms[mrot])
            yield

    out["p3"] = [sms[0].tok(), sms[1].tok(), sv.tok(), ss.tok(), stm.tok(),
                 (sce, sce.val), (scf, scf.val), (sxn, sxn.val)]


def emit_p6(p, sb, dr, prev_toks, out):
    sl6 = [p.sem(f"p6l{i}", dma=True) for i in range(2)]
    sbc = p.sem("p6bc", dma=True)
    sos = [p.sem(f"p6o{i}", dma=True) for i in range(2)]
    sv = p.sem("p6v")
    ss = p.sem("p6s")

    p.dma("gpsimd", sb["mbc"][:], dr["mrow"][:, 0].partition_broadcast(128),
          waits=list(prev_toks), inc=sbc)
    p.dma("gpsimd", sb["stdbc"][:], dr["stdrow"][:, 0].partition_broadcast(128),
          inc=sbc)
    buf_use = [None, None]
    g6_use = None
    for dc in range(DC):
        bi = dc % 2
        p.dma("sync", sb["s6"][:, bi], dr["s_fm"][dc],
              waits=[buf_use[bi]] + (list(prev_toks) if dc < 2 else []),
              inc=sl6[bi])
        p.dma("sync", sb["m6"][:, bi], dr["moe_fm"][dc], inc=sl6[bi])
        p.dma_t("sync", sb["xn6h"][:, bi], dr["xn_hi"][:, dc * 128:(dc + 1) * 128],
                inc=sl6[bi])
        p.dma_t("sync", sb["xn6l"][:, bi], dr["xn_lo"][:, dc * 128:(dc + 1) * 128],
                inc=sl6[bi])
        t_g = p.act(sb["g6"][:], sb["m6"][:, bi], AF.Silu,
                    waits=[(sl6[bi], sl6[bi].val), g6_use], inc=ss)
        t_1 = p.tt(sb["t6"][:], sb["xn6h"][:, bi], sb["xn6l"][:, bi], ALU.add,
                   waits=[(sl6[bi], sl6[bi].val), (sbc, sbc.val)], inc=sv)
        p.tt(sb["t6"][:], sb["t6"][:], sb["stdbc"][:], ALU.mult)
        p.tt(sb["t6"][:], sb["t6"][:], sb["mbc"][:], ALU.add)
        t_4 = p.tt(sb["u6"][:], sb["s6"][:, bi], sb["g6"][:], ALU.mult,
                   waits=[t_g], inc=sv)
        g6_use = t_4
        t_o = p.tt(sb["o6"][:, bi], sb["t6"][:], sb["u6"][:], ALU.add,
                   waits=[(sos[bi], sos[bi].val)], inc=sv)
        buf_use[bi] = t_o
        p.dma("gpsimd", dr["out_fm"][dc], sb["o6"][:, bi], waits=[t_o], inc=sos[bi])
        yield

    out["p6"] = [sos[0].tok(), sos[1].tok()]



def build_nc(dev=False):
    nc = bass.Bass("TRN2", target_bir_lowering=False, detect_race_conditions=False)
    p = Prog()

    dr = {}
    dr["x_ext"] = nc.dram_tensor("x_ext", [TEXT, D], F32, kind="ExternalInput").ap()
    dr["w_qkv"] = nc.dram_tensor("w_qkv", [3 * D, D], F32, kind="ExternalInput").ap()
    dr["w_sw"] = nc.dram_tensor("w_sw", [2 * DFF, D], F32, kind="ExternalInput").ap()
    dr["w_out"] = nc.dram_tensor("w_out", [D, DFF], F32, kind="ExternalInput").ap()
    dr["w_r"] = nc.dram_tensor("w_r", [D, NE], F32, kind="ExternalInput").ap()
    dr["w_gate"] = nc.dram_tensor("w_gate", [NE, D, FMOE], F32, kind="ExternalInput").ap()
    dr["w_up"] = nc.dram_tensor("w_up", [NE, D, FMOE], F32, kind="ExternalInput").ap()
    dr["w_down"] = nc.dram_tensor("w_down", [NE, FMOE, D], F32, kind="ExternalInput").ap()
    dr["maskh"] = nc.dram_tensor("maskh", [128, 1], F32, kind="ExternalInput").ap()
    dr["iota16"] = nc.dram_tensor("iota16", [128, NE], F32, kind="ExternalInput").ap()
    dr["eye128"] = nc.dram_tensor("eye128", [128, 128], F32, kind="ExternalInput").ap()

    dr["out_fm"] = nc.dram_tensor("out_fm", [DC, 128, TOK], F32, kind="ExternalOutput").ap()

    def scr(name, shape, dt):
        if name in DEV_OUT:
            dr[name] = nc.dram_tensor(name, shape, dt, kind="ExternalOutput").ap()
        else:
            dr[name] = nc.dram_tensor(name, shape, dt).ap()
        return dr[name]

    scr("xr_hi", [TEXT, D], BF16)
    scr("xr_lo", [TEXT, D], BF16)
    scr("xn_hi", [TOK, D], BF16)
    scr("xn_lo", [TOK, D], BF16)
    scr("mrow", [TOK, 1], F32)
    scr("stdrow", [TOK, 1], F32)
    scr("rows", [8, TEXT], F32)
    scr("qkvT_hi", [48, 128, DC, 128], BF16)
    scr("qkvT_lo", [48, 128, DC, 128], BF16)
    scr("swN_hi", [2 * DFF, D], BF16)
    scr("swN_lo", [2 * DFF, D], BF16)
    scr("woN_hi", [D, DFF], BF16)
    scr("woN_lo", [D, DFF], BF16)
    scr("gateN_hi", [NE, D, FMOE], BF16)
    scr("gateN_lo", [NE, D, FMOE], BF16)
    scr("upN_hi", [NE, D, FMOE], BF16)
    scr("upN_lo", [NE, D, FMOE], BF16)
    scr("downN_hi", [NE, FMOE, D], BF16)
    scr("downN_lo", [NE, FMOE, D], BF16)
    scr("y_hi", [DC, 128, TOK], BF16)
    scr("y_lo", [DC, 128, TOK], BF16)
    scr("h_hi", [64, 128, TOK], BF16)
    scr("h_lo", [64, 128, TOK], BF16)
    scr("s_fm", [DC, 128, TOK], F32)
    scr("moe_fm", [DC, 128, TOK], F32)
    scr("c_fm", [NE, TOK], F32)
    scr("dbg", [16, 128, 24], F32)

    with contextlib.ExitStack() as es:
        sb = {}
        ARENA = 204 * 1024
        arena_base = (nc.sbuf_base + 31) // 32 * 32
        nc.alloc_sbuf_tensor("arena", [128, ARENA + 64], mybir.dt.uint8)

        def place(name, shape, dt, off):
            assert off % 32 == 0, name
            t = nc.alloc_sbuf_tensor_at(name, shape, dt, offset=arena_base + off)
            nbytes = int(np.prod(shape[1:])) * (2 if dt == BF16 else 4)
            assert off + nbytes <= ARENA, (name, off, nbytes)
            sb[name] = t
            return t

        G = 0
        PREP = 512
        R = 25088

        place("ones1", [128, 1], F32, G + 0)
        place("rowst", [128, 16], F32, G + 32)
        place("carry", [128, 2, DC], F32, G + 96)
        place("ctmp", [128, 8], F32, G + 224)
        place("maskc", [128, 1], F32, G + 288)

        place("wslab", [128, 2, D], F32, PREP + 0)
        place("wshi", [128, D], BF16, PREP + 16384)
        place("wslo", [128, D], BF16, PREP + 20480)
        place("xbout", [128, 2, DC, 128], BF16, R + 165888)

        place("xt", [128, 2, D], F32, R + 0)
        place("tmpa", [128, D], F32, R + 16384)
        place("tmpb", [128, D], F32, R + 24576)
        place("hsta", [128, 2, 2, D], BF16, R + 32768)
        place("lsta", [128, 2, 2, D], BF16, R + 49152)
        place("stats", [128, 48], F32, R + 65536)
        place("xrs_hi", [128, DC, 512], BF16, R + 0)
        place("xrs_lo", [128, DC, 512], BF16, R + 16384)
        place("wtile", [128, 2, 2, DC, 128], BF16, R + 32768)
        place("Sbuf", [128, DC, 512], F32, R + 49152)
        place("wbuf", [128, DC, 512], F32, R + 81920)
        place("wvbuf", [128, DC, 512], F32, R + 114688)
        place("sq512", [128, 512], F32, R + 147456)
        place("bc0", [128, 512], F32, R + 149504)
        place("bc1", [128, 512], F32, R + 153600)
        place("onesT", [128, TOK], F32, R + 157696)
        place("sc512a", [128, 512], F32, R + 161792)
        place("sc512b", [128, 512], F32, R + 163840)
        place("sc512c", [128, 512], F32, R + 165888)
        place("sc512d", [128, 512], F32, R + 167936)
        place("yst", [128, 2, 2, 512], BF16, R + 169984)
        place("ybuf_hi", [128, DC, 1024], BF16, R + 0)
        place("ybuf_lo", [128, DC, 1024], BF16, R + 32768)
        place("swtile", [128, 2, 2, 2, DC, 128], BF16, R + 65536)
        place("silug", [128, 1024], F32, R + 98304)
        place("ht1", [128, 1024], F32, R + 102400)
        place("hst", [128, 2, 2, 1024], BF16, R + 106496)
        place("oacc", [128, DC, 1024], F32, R + 0)
        place("h5_hi", [128, 2, 8, 1024], BF16, R + 65536)
        place("h5_lo", [128, 2, 8, 1024], BF16, R + 98304)
        place("wotile", [128, 2, 2, 8, 128], BF16, R + 131072)
        place("xn5h", [128, 2, 1024], BF16, R + 139264)
        place("xn5l", [128, 2, 1024], BF16, R + 143360)
        place("sfin", [128, 2, 1024], F32, R + 147456)
        place("xnf3h", [128, DC, 1024], BF16, R + 0)
        place("xnf3l", [128, DC, 1024], BF16, R + 32768)
        place("htil_hi", [128, 6, 1024], BF16, R + 65536)
        place("htil_lo", [128, 6, 1024], BF16, R + 77824)
        place("guh", [128, 2, 2, DC, 256], BF16, R + 90112)
        place("gul", [128, 2, 2, DC, 256], BF16, R + 122880)
        place("silug", [128, 1024], F32, R + 155648)
        place("mt1", [128, 1024], F32, R + 159744)
        place("cebc", [128, 1024], F32, R + 163840)
        place("macc", [128, 2, 1024], F32, R + 167936)
        place("dth", [128, 2, 6, 128], BF16, R + 176128)
        place("dtl", [128, 2, 6, 128], BF16, R + 179200)
        place("wrs", [128, DC, NE], F32, R + 143360)
        place("wrh", [128, DC, NE], BF16, R + 144384)
        place("wrl", [128, DC, NE], BF16, R + 144896)
        place("eyes", [128, 128], F32, R + 145408)
        place("iotas", [128, NE], F32, R + 145920)
        place("rtq", [128, 80], F32, R + 146016)
        place("cfm", [128, 1024], F32, R + 167936)
        place("s6", [128, 2, 1024], F32, R + 0)
        place("m6", [128, 2, 1024], F32, R + 8192)
        place("xn6h", [128, 2, 1024], BF16, R + 16384)
        place("xn6l", [128, 2, 1024], BF16, R + 20480)
        place("mbc", [128, 1024], F32, R + 24576)
        place("stdbc", [128, 1024], F32, R + 28672)
        place("g6", [128, 1024], F32, R + 32768)
        place("t6", [128, 1024], F32, R + 36864)
        place("u6", [128, 1024], F32, R + 40960)
        place("o6", [128, 2, 1024], F32, R + 45056)

        def psum(name, shape):
            sb[name] = es.enter_context(nc.psum_tensor(name, shape, F32))
        for i in range(4):
            psum(f"pA{i}", [128, 512])
        for i in range(2):
            psum(f"pB{i}", [128, 1024])

        marks, out = {}, {}
        gprep = emit_prep(p, sb, dr, marks)
        for _ in emit_p0(p, sb, dr, out):
            for _ in range(3):
                next(gprep, None)
        for _ in emit_p1(p, sb, dr, out["p0"], marks, out):
            pass
        for _ in emit_p3(p, sb, dr, out["p1"], gprep, marks, out):
            next(gprep, None)
        for _ in emit_p4(p, sb, dr, out["p3"], gprep, marks, out):
            next(gprep, None)
        for _ in emit_p5(p, sb, dr, out["p4"], gprep, marks, out):
            next(gprep, None)
        for _ in gprep:
            pass
        for _ in emit_p6(p, sb, dr, list(out["p5"]) + list(out["p3"]), out):
            pass
        p.wait("gpsimd", *out["p6"])

        for s in p.sems:
            s.h = es.enter_context(nc.semaphore(s.name))
        block = es.enter_context(nc.Block())

        for eng in ENGINES:
            def emit(eh, eng=eng):
                for f in p.q[eng]:
                    f(eh)
            getattr(block, eng)(emit)

    return nc



def kernel(L0_attn_x, L0_attn_w_qkv, L0_attn_w_swiglu, L0_attn_w_out,
           L0_ffn_router_w, L0_ffn_gate_w, L0_ffn_up_w, L0_ffn_down_w):
    x = np.ascontiguousarray(np.asarray(L0_attn_x, dtype=np.float32))
    wq = np.ascontiguousarray(np.asarray(L0_attn_w_qkv, dtype=np.float32))
    wsw = np.ascontiguousarray(np.asarray(L0_attn_w_swiglu, dtype=np.float32))
    wo = np.ascontiguousarray(np.asarray(L0_attn_w_out, dtype=np.float32))
    wr = np.ascontiguousarray(np.asarray(L0_ffn_router_w, dtype=np.float32))
    wg = np.ascontiguousarray(np.asarray(L0_ffn_gate_w, dtype=np.float32))
    wu = np.ascontiguousarray(np.asarray(L0_ffn_up_w, dtype=np.float32))
    wd = np.ascontiguousarray(np.asarray(L0_ffn_down_w, dtype=np.float32))

    iota16 = np.tile(np.arange(NE, dtype=np.float32), (128, 1))
    eye128 = np.eye(128, dtype=np.float32)

    nc = build_nc()
    in_maps = []
    for c in range(8):
        b, h = c // 2, c % 2
        own = x[b, h * TOK:(h + 1) * TOK]
        prev = x[b, 0:TOK] if h == 1 else own
        x_ext = np.concatenate([prev, own], axis=0)
        in_maps.append(dict(
            x_ext=x_ext, w_qkv=wq, w_sw=wsw, w_out=wo, w_r=wr,
            w_gate=wg, w_up=wu, w_down=wd,
            maskh=np.full((128, 1), float(h), np.float32),
            iota16=iota16, eye128=eye128,
        ))
    res = run_bass_kernel_spmd(nc, in_maps, core_ids=list(range(8)))
    out = np.empty((B, S, D), np.float32)
    for c in range(8):
        b, h = c // 2, c % 2
        o = np.asarray(res.results[c]["out_fm"])
        out[b, h * TOK:(h + 1) * TOK, :] = o.reshape(D, TOK).T
    return out
